# version 05 (final)
# baseline (speedup 1.0000x reference)
import math
from collections import defaultdict
from contextlib import ExitStack
from types import SimpleNamespace

import numpy as np
import ml_dtypes

import concourse.bass as bass
import concourse.mybir as mybir
from concourse.bass_utils import run_bass_kernel_spmd

F32 = mybir.dt.float32
BF16 = mybir.dt.bfloat16
AF = mybir.ActivationFunctionType
ALU = mybir.AluOpType
AX = mybir.AxisListType

S = 4096
HALF = 2048
D = 2048
H_MY = 8
NOPE, ROPE, VD = 128, 64, 128
QL, KVL = 768, 512
QKD = NOPE + ROPE
SM_SCALE = QKD ** -0.5
RMS_EPS = 1e-6
LN_EPS = 1e-5
NH_MY = 32
HD = 64
MYI = NH_MY * HD
MYG = 4
NST = 128
MYGN = MYG * NST
CONVD_MY = MYI + 2 * MYGN
INTER = 4096
MAMBA_EPS = 1e-5
NTILE = 128
NT_FULL = S // NTILE
NT_HALF = HALF // NTILE

STAGE_LIMIT = None
S9_DEBUG = False
RG_OVERRIDE = None


class Emitter:
    ENGINES = ("sync", "tensor", "vector", "scalar", "gpsimd")

    def __init__(self):
        self.ops = {e: [] for e in self.ENGINES}
        self.counts = defaultdict(int)


def build_program(debug_dumps=()):
    nc = bass.Bass("TRN2", target_bir_lowering=False)
    em = Emitter()
    c = SimpleNamespace(nc=nc, em=em)

    def di(name, shape):
        return nc.dram_tensor(name, list(shape), F32, kind="ExternalInput")

    c.x_full = di("x_full", (S, D))
    c.x_my = di("x_my", (HALF, D))
    c.w1 = di("w1", (D, QL + ROPE + KVL))
    c.g_q = di("g_q", (128, QL // 128))
    c.g_kv = di("g_kv", (128, KVL // 128))
    c.w_uq_nope = di("w_uq_nope", (QL, H_MY * NOPE))
    c.w_uq_pe = di("w_uq_pe", (QL, H_MY * ROPE))
    c.w_uk = di("w_uk", (KVL, H_MY * NOPE))
    c.w_uv = di("w_uv", (KVL, H_MY * VD))
    c.w_o_my = di("w_o_my", (H_MY * VD, D))
    c.ffn_w = di("ffn_w", (D, D))
    c.ffn_b = di("ffn_b", (128, D))
    c.wz_my = di("wz_my", (D, MYI))
    c.wx_my = di("wx_my", (D, MYI))
    c.wb_my = di("wb_my", (D, MYGN))
    c.wc_my = di("wc_my", (D, MYGN))
    c.wdt_my = di("wdt_my", (D, NH_MY))
    c.conv_w_my = di("conv_w_my", (CONVD_MY, 4))
    c.conv_b_my = di("conv_b_my", (128, CONVD_MY // 128))
    c.dt_bias_t = di("dt_bias_t", (128, NH_MY))
    c.a_log_t = di("a_log_t", (128, NH_MY))
    c.d_coef_t = di("d_coef_t", (128, NH_MY))
    c.norm_w_my = di("norm_w_my", (128, MYI // 128))
    c.wout_my = di("wout_my", (MYI, D))
    c.c_ident = di("c_ident", (128, 128))
    c.c_causal = di("c_causal", (128, 128))
    c.c_uincl = di("c_uincl", (128, 128))
    c.c_slow = di("c_slow", (128, 128))
    c.c_negid = di("c_negid", (128, 512))
    c.c_cos = di("c_cos", (S, 32))
    c.c_sin = di("c_sin", (S, 32))

    c.out = nc.dram_tensor("out", [HALF, D], F32, kind="ExternalOutput")

    def dsc(name, shape, dtype=BF16):
        t = nc.dram_tensor(name, list(shape), dtype)
        setattr(c, name, t)
        return t

    dsc("d_lnx_fm", (D, S))
    dsc("d_rmsq_fm", (QL, S))
    dsc("d_rmskv_fm", (KVL, S))
    dsc("d_kpe_fm", (64, S))
    dsc("d_qpe_fm", (H_MY * ROPE, S))
    dsc("d_qnope_fm", (H_MY * NOPE, S))
    dsc("d_knope_fm", (H_MY * NOPE, S))
    dsc("d_v_tm", (H_MY, S // 128, 128, VD))
    dsc("d_feat_fm", (H_MY * VD, S))
    dsc("d_bwT", (D, D))
    dsc("d_r1", (HALF, D), F32)
    dsc("d_r2", (HALF, D), F32)
    dsc("d_z", (S, MYI))
    dsc("d_y", (S, MYI), F32)
    dsc("d_g", (S, MYI), F32)
    dsc("d_xbc", (CONVD_MY, S))
    dsc("d_xbcs", (CONVD_MY, S))
    dsc("rs_attn_in", (S, D), F32)
    dsc("rs_attn_out", (HALF, D), F32)
    dsc("ag_ln3_in", (HALF, D))
    dsc("ag_ln3_out", (S, D))
    dsc("ar_var_in", (S, 1), F32)
    dsc("ar_var_out", (S, 1), F32)
    dsc("rs_mam_in", (S, D), F32)
    dsc("rs_mam_out", (HALF, D), F32)

    c.RG = RG_OVERRIDE if RG_OVERRIDE is not None else [[0, 1], [2, 3], [4, 5], [6, 7]]
    c.debug_dumps = list(debug_dumps)
    c.dbg_outs = {}
    for name, rows, cols in c.debug_dumps:
        src = getattr(c, name)
        odt = F32
        c.dbg_outs[name] = nc.dram_tensor(
            "dbg_" + name, [rows, cols], odt, kind="ExternalOutput")

    with ExitStack() as es:
        E = es.enter_context
        c.ab1 = E(nc.sbuf_tensor("ab1", [128, 32768], BF16))
        c.ab2 = E(nc.sbuf_tensor("ab2", [128, 24576], BF16))
        c.af1 = E(nc.sbuf_tensor("af1", [128, 8192], F32))
        c.af2 = E(nc.sbuf_tensor("af2", [128, 4096], F32))
        c.af3 = E(nc.sbuf_tensor("af3", [128, 4096], F32))
        c.sm = E(nc.sbuf_tensor("sm", [128, 512], F32))
        c.sm2 = E(nc.sbuf_tensor("sm2", [128, 512], F32))
        c.cstb = E(nc.sbuf_tensor("cstb", [128, 1024], BF16))
        c.cstf = E(nc.sbuf_tensor("cstf", [128, 1280], F32))
        c.p0 = E(nc.psum_tensor("p0", [128, 2048], F32))
        c.p4 = E(nc.psum_tensor("p4", [128, 512], F32))
        c.p5 = E(nc.psum_tensor("p5", [128, 512], F32))
        c.pt6 = E(nc.psum_tensor("pt6", [128, 1024], BF16))
        c.pt7 = E(nc.psum_tensor("pt7", [128, 1024], BF16))
        c.sems = {}
        for s_name in ("ld", "ldk", "ldw0", "ldw1", "ldx0", "ldx1", "ldc0", "ldc1", "st", "mm", "dve", "act", "pool", "ccs"):
            c.sems[s_name] = E(nc.semaphore(s_name))
        block = E(nc.Block())

        _emit_all(c)

        for eng_name in Emitter.ENGINES:
            def make(engn):
                def body(eng):
                    for fn in em.ops[engn]:
                        fn(eng)
                return body
            getattr(block, eng_name)(make(eng_name))

    return nc



def _mk_helpers(c):
    em, sems = c.em, c.sems

    def W(eng, sem, val=None):
        if val is None:
            val = em.counts[sem]
        em.ops[eng].append(lambda e, s=sems[sem], v=val: e.wait_ge(s, v))

    def on(eng, fn, inc=None, amt=1):
        if inc is None:
            em.ops[eng].append(fn)
            return em.counts.get(inc, 0)
        em.counts[inc] += amt
        tgt = em.counts[inc]
        em.ops[eng].append(lambda e, f=fn, s=sems[inc], a=amt: f(e).then_inc(s, a))
        return tgt

    def val(sem):
        return em.counts[sem]

    def VS(fn):
        t = on("vector", fn, inc="dve")
        W("vector", "dve", t)
        return t

    def MK(eng, sem):
        em.counts[sem] += 1
        tgt = em.counts[sem]
        prev = em.ops[eng].pop()
        em.ops[eng].append(lambda e, p=prev, s=sems[sem]: p(e).then_inc(s, 1))
        return tgt

    c.W, c.on, c.val, c.VS, c.MK = W, on, val, VS, MK


def _emit_all(c):
    _mk_helpers(c)
    emit_s0(c)
    if STAGE_LIMIT == "smoke":
        W, on = c.W, c.on
        W("gpsimd", "ldk")
        on("gpsimd", lambda e: e.dma_start(
            out=c.d_r1[0:128, 0:128], in_=c.cstf[:, 0:128]), inc="st", amt=16)
        W("gpsimd", "dve")
        on("gpsimd", lambda e: e.dma_start(
            out=c.d_lnx_fm[0:128, 0:128], in_=c.cstb[:, 0:128]), inc="st", amt=16)
        _finish(c)
        return
    if STAGE_LIMIT == "s0":
        _finish(c)
        return
    if STAGE_LIMIT == "s1mini":
        emit_s1(c, nstrips=1, ntiles=1)
        W, on = c.W, c.on
        W("gpsimd", "dve")
        W("gpsimd", "act")
        on("gpsimd", lambda e: e.dma_start(
            out=c.d_r1[0:128, 0:32], in_=c.sm[:, 0:32]), inc="st", amt=16)
        on("gpsimd", lambda e: e.dma_start(
            out=c.d_z[0:128, 0:2048], in_=c.ab1[:, 0:2048]), inc="st", amt=16)
        on("gpsimd", lambda e: e.dma_start(
            out=c.d_r2[0:128, 0:2048], in_=c.af1[:, 0:2048]), inc="st", amt=16)
        _finish(c)
        return
    emit_s1(c)
    if STAGE_LIMIT == "s1":
        _finish(c)
        return
    emit_s0b(c)
    if STAGE_LIMIT == "s0b":
        _finish(c)
        return
    emit_s2a(c)
    if STAGE_LIMIT == "s2a":
        _finish(c)
        return
    emit_s2b(c)
    if STAGE_LIMIT == "s2b":
        _finish(c)
        return
    emit_s2c(c)
    if STAGE_LIMIT == "s2c":
        _finish(c)
        return
    emit_s3(c)
    if STAGE_LIMIT == "s3":
        _finish(c)
        return
    emit_s5(c)
    if STAGE_LIMIT in ("s5", "s5dbg"):
        _finish(c)
        return
    emit_s6(c)
    if STAGE_LIMIT == "s6":
        _finish(c)
        return
    emit_s7(c)
    if STAGE_LIMIT == "s7":
        c.W("gpsimd", "dve")
        c.on("gpsimd", lambda e: e.dma_start(
            out=c.d_g[0:128, 0:2048], in_=c.af3[:, 0:2048]), inc="st", amt=16)
        _finish(c)
        return
    emit_s8(c)
    if STAGE_LIMIT == "s8":
        _finish(c)
        return
    emit_s9(c)
    _finish(c)


def _finish(c):
    W, on = c.W, c.on
    for s_name in ("mm", "dve", "act", "ld", "ldk", "ldw0", "ldw1", "ldx0", "ldx1", "ldc0", "ldc1", "ccs"):
        W("gpsimd", s_name)
    for name, rows, cols in c.debug_dumps:
        src = getattr(c, name)
        dst = c.dbg_outs[name]
        if src.dtype == F32:
            for r0 in range(0, rows, 128):
                nr = min(128, rows - r0)
                on("gpsimd", lambda e, s=src, d=dst, r=r0, nr=nr, cols=cols: e.dma_start(
                    out=d[r:r + nr, 0:cols], in_=s[r:r + nr, 0:cols]), inc="st", amt=16)
        else:
            for r0 in range(0, rows, 128):
                nr = min(128, rows - r0)
                t_in = on("gpsimd", lambda e, s=src, r=r0, nr=nr, cols=cols: e.dma_start(
                    out=c.ab1[0:nr, 0:cols], in_=s[r:r + nr, 0:cols]), inc="st", amt=16)
                W("vector", "st", t_in)
                t_cast = on("vector", lambda e, nr=nr, cols=cols: e.tensor_copy(
                    c.af1[0:nr, 0:cols], c.ab1[0:nr, 0:cols]), inc="dve")
                W("gpsimd", "dve", t_cast)
                t_out = on("gpsimd", lambda e, d=c.dbg_outs[name], r=r0, nr=nr, cols=cols: e.dma_start(
                    out=d[r:r + nr, 0:cols], in_=c.af1[0:nr, 0:cols]), inc="st", amt=16)
                W("gpsimd", "st", t_out)
    W("gpsimd", "st")



def emit_s0(c):
    W, on = c.W, c.on
    sy = "sync"

    loads = [
        (c.c_ident, 0, 128), (c.c_causal, 128, 128), (c.c_uincl, 256, 128),
        (c.c_slow, 384, 128), (c.c_negid, 512, 512),
    ]
    for src, col, w in loads:
        on(sy, lambda e, s=src, cl=col, ww=w: e.dma_start(
            out=c.cstf[:, cl:cl + ww], in_=s[:, :]), inc="ldk", amt=16)
    on(sy, lambda e: e.dma_start(out=c.cstf[:, 1024:1030], in_=c.g_q[:, :]), inc="ldk", amt=16)
    on(sy, lambda e: e.dma_start(out=c.cstf[:, 1030:1034], in_=c.g_kv[:, :]), inc="ldk", amt=16)
    on(sy, lambda e: e.dma_start(out=c.cstf[:, 1034:1066], in_=c.dt_bias_t[:, :]), inc="ldk", amt=16)
    on(sy, lambda e: e.dma_start(out=c.cstf[:, 1066:1098], in_=c.a_log_t[:, :]), inc="ldk", amt=16)
    on(sy, lambda e: e.dma_start(out=c.cstf[:, 1098:1130], in_=c.d_coef_t[:, :]), inc="ldk", amt=16)
    on(sy, lambda e: e.dma_start(out=c.cstf[:, 1146:1170], in_=c.conv_b_my[:, :]), inc="ldk", amt=16)
    on(sy, lambda e: e.dma_start(out=c.cstf[:, 1170:1186], in_=c.norm_w_my[:, :]), inc="ldk", amt=16)
    W("vector", "ldk")
    on("vector", lambda e: e.tensor_copy(c.cstb[:, 0:1024], c.cstf[:, 0:1024]), inc="dve")
    c.CIDENT = lambda: c.cstb[:, 0:128]
    c.CCAUSAL = lambda: c.cstb[:, 128:256]
    c.CUINCL = lambda: c.cstb[:, 256:384]
    c.CSLOW = lambda: c.cstb[:, 384:512]
    c.CNEGID = lambda: c.cstb[:, 512:1024]
    c.GQ = lambda j: c.cstf[:, 1024 + j:1025 + j]
    c.GKV = lambda j: c.cstf[:, 1030 + j:1031 + j]
    c.DTBIAS = lambda: c.cstf[:, 1034:1066]
    c.ALOG = lambda: c.cstf[:, 1066:1098]
    c.DCOEF = lambda: c.cstf[:, 1098:1130]
    c.CONVB = lambda j: c.cstf[:, 1146 + j:1147 + j]
    c.NORMW = lambda j: c.cstf[:, 1170 + j:1171 + j]
    c.s0_const_dve = c.val("dve")



def emit_s1(c, nstrips=8, ntiles=4):
    W, on = c.W, c.on

    W1C = QL + ROPE + KVL
    for f in range(16):
        buf = (f % 2) * W1C
        lsem = "ldw%d" % (f % 2)
        if f >= 2:
            W("sync", "dve")
        on("sync", lambda e, f=f, buf=buf: e.dma_start(
            out=c.af2[:, buf:buf + W1C], in_=c.w1[f * 128:(f + 1) * 128, :]),
            inc=lsem, amt=16)
        W("vector", lsem)
        on("vector", lambda e, f=f, buf=buf: e.tensor_copy(
            c.ab2[:, f * W1C:(f + 1) * W1C], c.af2[:, buf:buf + W1C]), inc="dve")
    w1_ready = c.val("dve")

    A_LNTM = 0
    A_SFM = 4096
    A_RMSQ = 12288
    A_RMSKV = 13824
    A_KPE = 14848
    A_SQ = 14976
    A_SKV = 18048
    A_SKPE = 20096
    A_JUNK = 20608

    (SM_SX, SM_SX2, SM_M, SM_T1, SM_MSQ, SM_VAR, SM_REC, SM_RSTD,
     SM_QS, SM_QR, SM_KS, SM_KR) = [4 * i for i in range(12)]

    last_evac_act = [c.val("act")]
    last_proj_done = [c.val("dve")]
    ln_done_by_pp = [0, 0]
    rope_done_by_pp = [0, 0]
    lntr_done_by_pp = [0, 0]
    rmstr_done_by_pp = [0, 0]

    for strip in range(nstrips):
        for ti in range(ntiles):
            t0 = strip * 512 + ti * 128
            pp = ti % 2
            lntm = lambda pp=pp: c.ab1[:, A_LNTM + pp * 2048:A_LNTM + (pp + 1) * 2048]
            xf = lambda pp=pp: c.af1[:, pp * 2048:(pp + 1) * 2048]

            if ln_done_by_pp[pp]:
                W("sync", "dve", ln_done_by_pp[pp])
            sx = "ldx%d" % pp
            sc_sem = "ldc%d" % pp
            ld_x = on("sync", lambda e, t0=t0, pp=pp: e.dma_start(
                out=c.af1[:, pp * 2048:(pp + 1) * 2048],
                in_=c.x_full[t0:t0 + 128, :]), inc=sx, amt=16)
            if rope_done_by_pp[pp]:
                W("sync", "dve", rope_done_by_pp[pp])
            on("sync", lambda e, t0=t0, pp=pp: e.dma_start(
                out=c.sm2[:, pp * 64:pp * 64 + 32], in_=c.c_cos[t0:t0 + 128, :]),
                inc=sc_sem, amt=16)
            ld_cs = on("sync", lambda e, t0=t0, pp=pp: e.dma_start(
                out=c.sm2[:, pp * 64 + 32:pp * 64 + 64], in_=c.c_sin[t0:t0 + 128, :]),
                inc=sc_sem, amt=16)

            W("vector", sx, ld_x)
            on("vector", lambda e, xf=xf: e.reduce_sum(
                c.sm[:, SM_SX:SM_SX + 1], xf(), axis=AX.X), inc="dve")
            W("scalar", sx, ld_x)
            sq_done = on("scalar", lambda e, xf=xf: e.activation(
                c.ab1[:, A_JUNK:A_JUNK + 2048], xf(), AF.Square,
                accum_out=c.sm[:, SM_SX2:SM_SX2 + 1]), inc="act")
            W("vector", "act", sq_done)
            c.VS(lambda e: e.tensor_scalar(
                c.sm[:, SM_M:SM_M + 1], c.sm[:, SM_SX:SM_SX + 1], 1.0 / D, None, ALU.mult))
            c.VS(lambda e: e.tensor_scalar(
                c.sm[:, SM_T1:SM_T1 + 1], c.sm[:, SM_SX2:SM_SX2 + 1], 1.0 / D, LN_EPS,
                ALU.mult, ALU.add))
            c.VS(lambda e: e.tensor_tensor(
                c.sm[:, SM_MSQ:SM_MSQ + 1], c.sm[:, SM_M:SM_M + 1],
                c.sm[:, SM_M:SM_M + 1], ALU.mult))
            c.VS(lambda e: e.tensor_tensor(
                c.sm[:, SM_VAR:SM_VAR + 1], c.sm[:, SM_T1:SM_T1 + 1],
                c.sm[:, SM_MSQ:SM_MSQ + 1], ALU.subtract))
            rec_done = c.VS(lambda e: e.reciprocal(
                c.sm[:, SM_REC:SM_REC + 1], c.sm[:, SM_VAR:SM_VAR + 1]))
            W("scalar", "dve", rec_done)
            rstd_done = on("scalar", lambda e: e.sqrt(
                c.sm[:, SM_RSTD:SM_RSTD + 1], c.sm[:, SM_REC:SM_REC + 1]), inc="act")
            W("vector", "act", rstd_done)
            if lntr_done_by_pp[pp]:
                W("vector", "mm", lntr_done_by_pp[pp])
            c.VS(lambda e, xf=xf: e.tensor_scalar(
                c.af2[:, 0:2048], xf(), c.sm[:, SM_M:SM_M + 1], None, ALU.subtract))
            ln_done = c.VS(lambda e, lntm=lntm: e.tensor_scalar(
                lntm(), c.af2[:, 0:2048], c.sm[:, SM_RSTD:SM_RSTD + 1], None,
                ALU.mult))

            W("tensor", "dve", ln_done)
            W("tensor", "act", last_evac_act[0])
            for f in range(16):
                pt = c.pt6 if f < 8 else c.pt7
                fo = (f % 8) * 128
                tr = on("tensor", lambda e, f=f, pt=pt, fo=fo, lntm=lntm: e.transpose(
                    pt[:, fo:fo + 128], lntm()[:, f * 128:(f + 1) * 128], c.CIDENT()),
                    inc="mm" if f in (7, 15) else None)
            mm_tr = c.val("mm")
            lntr_done_by_pp[pp] = mm_tr
            W("scalar", "mm", mm_tr - 1)
            for half, pt in ((0, c.pt6), (1, c.pt7)):
                if half == 1:
                    W("scalar", "mm", mm_tr)
                dst = c.ab1[:].rearrange("p (a f) -> p a f", f=512)
                outap = bass.AP(
                    c.ab1, A_SFM + half * 8 * 512 + ti * 128,
                    [[32768, 128], [512, 8], [1, 128]])
                ev = on("scalar", lambda e, pt=pt, o=outap: e.copy(o, pt[:, :]), inc="act")
            last_evac_act[0] = c.val("act")

            W("tensor", "act", last_evac_act[0])
            W("tensor", "dve", max(w1_ready, last_proj_done[0]))
            chunks = [(0, 512, 0, True, True), (512, 256, 512, True, False),
                      (768, 64, 768, False, True), (832, 512, 1024, True, True)]
            for f in range(16):
                lh = lambda f=f, ti=ti: c.ab1[:, A_SFM + f * 512 + ti * 128:
                                              A_SFM + f * 512 + ti * 128 + 128]
                for ci, (wc, wwidth, pc, bfirst, blast) in enumerate(chunks):
                    last = (f == 15 and ci == len(chunks) - 1)
                    on("tensor", lambda e, f=f, wc=wc, ww=wwidth, pc=pc, lh=lh,
                       bf=bfirst, bl=blast: e.matmul(
                        c.p0[:, pc:pc + ww], lh(),
                        c.ab2[:, f * W1C + wc:f * W1C + wc + ww],
                        start=(f == 0 and bf), stop=(f == 15 and bl)),
                        inc="mm" if last else None)
            proj_mm = c.val("mm")

            W("scalar", "mm", proj_mm)
            qs = on("scalar", lambda e: e.activation(
                c.ab1[:, A_JUNK:A_JUNK + 768], c.p0[:, 0:768], AF.Square,
                accum_out=c.sm[:, SM_QS:SM_QS + 1]), inc="act")
            ks = on("scalar", lambda e: e.activation(
                c.ab1[:, A_JUNK + 768:A_JUNK + 1280], c.p0[:, 1024:1536], AF.Square,
                accum_out=c.sm[:, SM_KS:SM_KS + 1]), inc="act")
            W("vector", "act", ks)
            c.VS(lambda e: e.tensor_scalar(
                c.sm[:, SM_QS + 1:SM_QS + 2], c.sm[:, SM_QS:SM_QS + 1], 1.0 / QL,
                RMS_EPS, ALU.mult, ALU.add))
            c.VS(lambda e: e.reciprocal(
                c.sm[:, SM_QR:SM_QR + 1], c.sm[:, SM_QS + 1:SM_QS + 2]))
            c.VS(lambda e: e.tensor_scalar(
                c.sm[:, SM_KS + 1:SM_KS + 2], c.sm[:, SM_KS:SM_KS + 1], 1.0 / KVL,
                RMS_EPS, ALU.mult, ALU.add))
            kr = c.VS(lambda e: e.reciprocal(
                c.sm[:, SM_KR:SM_KR + 1], c.sm[:, SM_KS + 1:SM_KS + 2]))
            W("scalar", "dve", kr)
            qr2 = on("scalar", lambda e: e.sqrt(
                c.sm[:, SM_QR + 1:SM_QR + 2], c.sm[:, SM_QR:SM_QR + 1]), inc="act")
            kr2 = on("scalar", lambda e: e.sqrt(
                c.sm[:, SM_KR + 1:SM_KR + 2], c.sm[:, SM_KR:SM_KR + 1]), inc="act")
            W("vector", "act", kr2)
            if rmstr_done_by_pp[pp]:
                W("vector", "mm", rmstr_done_by_pp[pp])
            on("vector", lambda e, pp=pp: e.tensor_scalar(
                c.ab1[:, A_RMSQ + pp * 768:A_RMSQ + (pp + 1) * 768], c.p0[:, 0:768],
                c.sm[:, SM_QR + 1:SM_QR + 2], None, ALU.mult))
            on("vector", lambda e, pp=pp: e.tensor_scalar(
                c.ab1[:, A_RMSKV + pp * 512:A_RMSKV + (pp + 1) * 512], c.p0[:, 1024:1536],
                c.sm[:, SM_KR + 1:SM_KR + 2], None, ALU.mult))
            kp = lambda: c.p0[:, 768:832]
            cosv = lambda pp=pp: c.sm2[:, pp * 64:pp * 64 + 32]
            sinv = lambda pp=pp: c.sm2[:, pp * 64 + 32:pp * 64 + 64]
            kro = lambda pp=pp: c.ab1[:, A_KPE + pp * 64:A_KPE + (pp + 1) * 64]
            rs = lambda: c.af2[:, 3000:3064]
            W("vector", sc_sem, ld_cs)
            on("vector", lambda e, kp=kp, cosv=cosv: e.tensor_tensor(
                rs()[:, 0:32], kp()[:, 0:32], cosv(), ALU.mult))
            c.VS(lambda e, kp=kp, sinv=sinv: e.tensor_tensor(
                rs()[:, 32:64], kp()[:, 32:64], sinv(), ALU.mult))
            c.VS(lambda e, kro=kro: e.tensor_tensor(
                kro()[:, 0:32], rs()[:, 0:32], rs()[:, 32:64], ALU.subtract))
            on("vector", lambda e, kp=kp, sinv=sinv: e.tensor_tensor(
                rs()[:, 0:32], kp()[:, 0:32], sinv(), ALU.mult))
            c.VS(lambda e, kp=kp, cosv=cosv: e.tensor_tensor(
                rs()[:, 32:64], kp()[:, 32:64], cosv(), ALU.mult))
            rope_done = c.VS(lambda e, kro=kro: e.tensor_tensor(
                kro()[:, 32:64], rs()[:, 0:32], rs()[:, 32:64], ALU.add))
            last_proj_done[0] = rope_done
            ln_done_by_pp[pp] = ln_done
            rope_done_by_pp[pp] = rope_done

            W("tensor", "dve", rope_done)
            W("tensor", "act", last_evac_act[0])
            for f in range(6):
                on("tensor", lambda e, f=f, pp=pp: e.transpose(
                    c.pt6[:, f * 128:(f + 1) * 128],
                    c.ab1[:, A_RMSQ + pp * 768 + f * 128:A_RMSQ + pp * 768 + (f + 1) * 128],
                    c.CIDENT()))
            for f in range(4):
                on("tensor", lambda e, f=f, pp=pp: e.transpose(
                    c.pt7[:, f * 128:(f + 1) * 128],
                    c.ab1[:, A_RMSKV + pp * 512 + f * 128:A_RMSKV + pp * 512 + (f + 1) * 128],
                    c.CIDENT()))
            tr2 = on("tensor", lambda e, pp=pp: e.transpose(
                c.pt7[0:64, 512:640],
                c.ab1[:, A_KPE + pp * 64:A_KPE + (pp + 1) * 64],
                c.CIDENT()), inc="mm")
            rmstr_done_by_pp[pp] = tr2
            W("scalar", "mm", tr2)
            ev1 = on("scalar", lambda e, ti=ti: e.copy(
                bass.AP(c.ab1, A_SQ + ti * 128, [[32768, 128], [512, 6], [1, 128]]),
                c.pt6[:, 0:768]), inc="act")
            ev2 = on("scalar", lambda e, ti=ti: e.copy(
                bass.AP(c.ab1, A_SKV + ti * 128, [[32768, 128], [512, 4], [1, 128]]),
                c.pt7[:, 0:512]), inc="act")
            ev3 = on("scalar", lambda e, ti=ti: e.copy(
                c.ab1[0:64, A_SKPE + ti * 128:A_SKPE + ti * 128 + 128],
                c.pt7[0:64, 512:640]), inc="act")
            last_evac_act[0] = c.val("act")

        W("gpsimd", "act", last_evac_act[0])
        s0c = strip * 512
        for f in range(16):
            on("gpsimd", lambda e, f=f, s0c=s0c: e.dma_start(
                out=c.d_lnx_fm[f * 128:(f + 1) * 128, s0c:s0c + 512],
                in_=c.ab1[:, A_SFM + f * 512:A_SFM + (f + 1) * 512]), inc="st", amt=16)
        for f in range(6):
            on("gpsimd", lambda e, f=f, s0c=s0c: e.dma_start(
                out=c.d_rmsq_fm[f * 128:(f + 1) * 128, s0c:s0c + 512],
                in_=c.ab1[:, A_SQ + f * 512:A_SQ + (f + 1) * 512]), inc="st", amt=16)
        for f in range(4):
            on("gpsimd", lambda e, f=f, s0c=s0c: e.dma_start(
                out=c.d_rmskv_fm[f * 128:(f + 1) * 128, s0c:s0c + 512],
                in_=c.ab1[:, A_SKV + f * 512:A_SKV + (f + 1) * 512]), inc="st", amt=16)
        st_tgt = on("gpsimd", lambda e, s0c=s0c: e.dma_start(
            out=c.d_kpe_fm[0:64, s0c:s0c + 512],
            in_=c.ab1[0:64, A_SKPE:A_SKPE + 512]), inc="st", amt=16)
        W("scalar", "st", st_tgt)
        W("vector", "st", st_tgt)




def emit_s0b(c):
    W, on, VS = c.W, c.on, c.VS
    for t in range(16):
        buf = (t % 2) * 2048
        lsem = "ldw%d" % (t % 2)
        W("sync", "dve")
        ldt = on("sync", lambda e, t=t, buf=buf: e.dma_start(
            out=c.af2[:, buf:buf + 2048], in_=c.ffn_w[t * 128:(t + 1) * 128, :]),
            inc=lsem, amt=16)
        W("vector", lsem, ldt)
        VS(lambda e, buf=buf: e.tensor_reduce(
            c.sm[:, 0:1], c.af2[:, buf:buf + 2048], axis=AX.X, op=ALU.add,
            apply_absolute_value=True))
        alpha_t = VS(lambda e: e.tensor_scalar(
            c.sm[:, 4:5], c.sm[:, 0:1], 1.0 / D, None, ALU.mult))
        W("scalar", "dve", alpha_t)
        sg = on("scalar", lambda e, buf=buf: e.sign(
            c.af1[:, 0:2048], c.af2[:, buf:buf + 2048]), inc="act")
        W("vector", "act", sg)
        bw_t = VS(lambda e: e.tensor_scalar(
            c.ab1[:, 0:2048], c.af1[:, 0:2048], c.sm[:, 4:5], None, ALU.mult))
        W("tensor", "dve", bw_t)
        W("tensor", "act")
        for fh, pt in ((0, c.pt6), (1, c.pt7)):
            for j in range(8):
                f = fh * 8 + j
                tr = on("tensor", lambda e, f=f, j=j, pt=pt: e.transpose(
                    pt[:, j * 128:(j + 1) * 128],
                    c.ab1[:, f * 128:(f + 1) * 128], c.CIDENT()),
                    inc="mm" if j == 7 else None)
        mm_t = c.val("mm")
        W("scalar", "mm", mm_t)
        ev1 = on("scalar", lambda e: e.copy(c.ab1[:, 2048:3072], c.pt6[:, :]), inc="act")
        ev2 = on("scalar", lambda e: e.copy(c.ab1[:, 3072:4096], c.pt7[:, :]), inc="act")
        W("gpsimd", "act", ev2)
        for f in range(16):
            src_col = 2048 + f * 128
            on("gpsimd", lambda e, f=f, t=t, sc=src_col: e.dma_start(
                out=c.d_bwT[f * 128:(f + 1) * 128, t * 128:(t + 1) * 128],
                in_=c.ab1[:, sc:sc + 128]), inc="st", amt=16)
        st_t = c.val("st")
        W("vector", "st", st_t)
        W("tensor", "st", st_t)



def emit_s2a(c):
    W, on, VS = c.W, c.on, c.VS
    specs = [
        (c.w_uq_nope, 6, 1024, 0, "GQ", SM_SCALE),
        (c.w_uq_pe, 6, 512, 6144, "GQ", SM_SCALE),
        (c.w_uk, 4, 1024, 9216, "GKV", 1.0),
        (c.w_uv, 4, 1024, 13312, "GKV", 1.0),
    ]
    slot = 0
    for wsrc, nk, wcols, base, gname, scl in specs:
        for f in range(nk):
            lsem = "ldw%d" % (slot % 2)
            buf = (slot % 2) * 2048
            W("sync", "dve")
            ldt = on("sync", lambda e, w=wsrc, f=f, buf=buf, wc=wcols: e.dma_start(
                out=c.af2[:, buf:buf + wc], in_=w[f * 128:(f + 1) * 128, :]),
                inc=lsem, amt=16)
            W("vector", lsem, ldt)
            gq = (c.GQ if gname == "GQ" else c.GKV)
            on("vector", lambda e, f=f, buf=buf, wc=wcols, base=base, g=gq,
               scl=scl: e.tensor_scalar(
                c.ab2[:, base + f * wc:base + (f + 1) * wc],
                c.af2[:, buf:buf + wc], g(f), scl, ALU.mult, ALU.mult), inc="dve")
            slot += 1
    w2_ready = c.val("dve")

    B_RQ, B_RKV, B_QP, B_VO, B_QN, B_KN, B_QPF = (
        0, 3072, 5120, 7168, 9216, 11264, 13312)
    last_strip_mm = [0]
    last_qn_evac = [0]
    prev_vo_dma = [0]
    prev_rope = [0]

    for strip in range(8):
        s0c = strip * 512
        lsem = "ldx%d" % (strip % 2)
        if last_strip_mm[0]:
            W("sync", "mm", last_strip_mm[0])
        W("sync", lsem)
        for f in range(6):
            on("sync", lambda e, f=f, s0c=s0c: e.dma_start(
                out=c.ab1[:, B_RQ + f * 512:B_RQ + (f + 1) * 512],
                in_=c.d_rmsq_fm[f * 128:(f + 1) * 128, s0c:s0c + 512]),
                inc=lsem, amt=16)
        for f in range(4):
            on("sync", lambda e, f=f, s0c=s0c: e.dma_start(
                out=c.ab1[:, B_RKV + f * 512:B_RKV + (f + 1) * 512],
                in_=c.d_rmskv_fm[f * 128:(f + 1) * 128, s0c:s0c + 512]),
                inc=lsem, amt=16)
        ld_strip = c.val(lsem)
        W("tensor", lsem, ld_strip)
        W("tensor", "dve", w2_ready)

        for which, base_w, nk, dstb, dst in (
                ("qn", 0, 6, B_QN, c.d_qnope_fm), ("kn", 9216, 4, B_KN, c.d_knope_fm)):
            rqb = B_RQ if which == "qn" else B_RKV
            for grp in range(2):
                if last_qn_evac[0]:
                    W("tensor", "act", last_qn_evac[0])
                for hh in range(4):
                    h = grp * 4 + hh
                    for f in range(nk):
                        on("tensor", lambda e, h=h, hh=hh, f=f, bw=base_w, rqb=rqb,
                           nk=nk: e.matmul(
                            c.p0[:, hh * 512:hh * 512 + 512],
                            c.ab2[:, bw + f * (1024 if nk == 4 or bw in (0, 9216) else 512) + h * 128:
                                  bw + f * (1024 if nk == 4 or bw in (0, 9216) else 512) + h * 128 + 128],
                            c.ab1[:, rqb + f * 512:rqb + (f + 1) * 512],
                            start=(f == 0), stop=(f == nk - 1)),
                            inc="mm" if (f == nk - 1 and hh == 3) else None)
                mm_t = c.val("mm")
                W("scalar", "mm", mm_t)
                W("scalar", "st")
                ev = on("scalar", lambda e, dstb=dstb: e.copy(
                    c.ab1[:, dstb:dstb + 2048], c.p0[:, 0:2048]), inc="act")
                last_qn_evac[0] = ev
                W("gpsimd", "act", ev)
                for hh in range(4):
                    h = grp * 4 + hh
                    on("gpsimd", lambda e, h=h, hh=hh, s0c=s0c, dstb=dstb, dst=dst:
                       e.dma_start(
                        out=dst[h * 128:(h + 1) * 128, s0c:s0c + 512],
                        in_=c.ab1[:, dstb + hh * 512:dstb + (hh + 1) * 512]),
                        inc="st", amt=16)

        for sub in range(4):
            t0 = s0c + sub * 128
            W("tensor", "dve", prev_rope[0])
            W("tensor", "st")
            for ch, pb in ((0, c.p4), (1, c.p5)):
                for f in range(4):
                    on("tensor", lambda e, ch=ch, pb=pb, f=f, sub=sub: e.matmul(
                        pb[:, 0:512],
                        c.ab1[:, B_RKV + f * 512 + sub * 128:B_RKV + f * 512 + sub * 128 + 128],
                        c.ab2[:, 13312 + f * 1024 + ch * 512:13312 + f * 1024 + ch * 512 + 512],
                        start=(f == 0), stop=(f == 3)),
                        inc="mm" if (f == 3 and ch == 1) else None)
            mm_v = c.val("mm")
            W("scalar", "mm", mm_v)
            if prev_vo_dma[0]:
                W("scalar", "st", prev_vo_dma[0])
            ev_v = on("scalar", lambda e: e.copy(c.ab1[:, B_VO:B_VO + 512], c.p4[:, 0:512]), inc="act")
            ev_v2 = on("scalar", lambda e: e.copy(c.ab1[:, B_VO + 512:B_VO + 1024], c.p5[:, 0:512]), inc="act")
            W("gpsimd", "act", ev_v2)
            for h in range(8):
                on("gpsimd", lambda e, h=h, t0=t0: e.dma_start(
                    out=c.d_v_tm[h, t0 // 128, :, :],
                    in_=c.ab1[:, B_VO + h * 128:B_VO + (h + 1) * 128]), inc="st", amt=16)
            prev_vo_dma[0] = c.val("st")

            W("tensor", "act", ev_v2)
            for f in range(6):
                on("tensor", lambda e, f=f, sub=sub: e.matmul(
                    c.p4[:, 0:512],
                    c.ab1[:, B_RQ + f * 512 + sub * 128:B_RQ + f * 512 + sub * 128 + 128],
                    c.ab2[:, 6144 + f * 512:6144 + (f + 1) * 512],
                    start=(f == 0), stop=(f == 5)),
                    inc="mm" if f == 5 else None)
            mm_qp = c.val("mm")
            csl = "ldc%d" % (sub % 2)
            W("sync", csl)
            if prev_rope[0]:
                W("sync", "dve", prev_rope[0])
            on("sync", lambda e, t0=t0, sub=sub: e.dma_start(
                out=c.sm2[:, (sub % 2) * 64:(sub % 2) * 64 + 32],
                in_=c.c_cos[t0:t0 + 128, :]), inc=csl, amt=16)
            ldc_t = on("sync", lambda e, t0=t0, sub=sub: e.dma_start(
                out=c.sm2[:, (sub % 2) * 64 + 32:(sub % 2) * 64 + 64],
                in_=c.c_sin[t0:t0 + 128, :]), inc=csl, amt=16)
            W("vector", csl, ldc_t)
            W("vector", "mm", mm_qp)
            q3 = c.p4[:, 0:512].rearrange("p (h z) -> p h z", h=8)
            r3a = c.af2[:, 0:256].rearrange("p (h z) -> p h z", h=8)
            r3b = c.af2[:, 256:512].rearrange("p (h z) -> p h z", h=8)
            o3 = c.ab1[:, B_QP + sub * 512:B_QP + (sub + 1) * 512].rearrange(
                "p (h z) -> p h z", h=8)
            cb = c.sm2[:, (sub % 2) * 64:(sub % 2) * 64 + 32].unsqueeze(1).broadcast_to((128, 8, 32))
            sb = c.sm2[:, (sub % 2) * 64 + 32:(sub % 2) * 64 + 64].unsqueeze(1).broadcast_to((128, 8, 32))
            on("vector", lambda e, q3=q3, r3a=r3a, cb=cb: e.tensor_tensor(
                r3a, q3[:, :, 0:32], cb, ALU.mult))
            VS(lambda e, q3=q3, r3b=r3b, sb=sb: e.tensor_tensor(
                r3b, q3[:, :, 32:64], sb, ALU.mult))
            VS(lambda e, o3=o3, r3a=r3a, r3b=r3b: e.tensor_tensor(
                o3[:, :, 0:32], r3a, r3b, ALU.subtract))
            on("vector", lambda e, q3=q3, r3a=r3a, sb=sb: e.tensor_tensor(
                r3a, q3[:, :, 0:32], sb, ALU.mult))
            VS(lambda e, q3=q3, r3b=r3b, cb=cb: e.tensor_tensor(
                r3b, q3[:, :, 32:64], cb, ALU.mult))
            rope_t = VS(lambda e, o3=o3, r3a=r3a, r3b=r3b: e.tensor_tensor(
                o3[:, :, 32:64], r3a, r3b, ALU.add))
            prev_rope[0] = rope_t
            W("tensor", "dve", rope_t)
            W("tensor", "act")
            for h in range(8):
                on("tensor", lambda e, h=h, sub=sub: e.transpose(
                    c.pt6[0:64, h * 128:(h + 1) * 128],
                    c.ab1[:, B_QP + sub * 512 + h * 64:B_QP + sub * 512 + (h + 1) * 64],
                    c.CIDENT()), inc="mm" if h == 7 else None)
            mm_qt = c.val("mm")
            W("scalar", "mm", mm_qt)
            outap = bass.AP(c.ab1, B_QPF + sub * 128, [[32768, 64], [512, 8], [1, 128]])
            on("scalar", lambda e, o=outap: e.copy(o, c.pt6[0:64, :]), inc="act")
        ev_l = c.val("act")
        W("gpsimd", "act", ev_l)
        for h in range(8):
            on("gpsimd", lambda e, h=h, s0c=s0c: e.dma_start(
                out=c.d_qpe_fm[h * 64:(h + 1) * 64, s0c:s0c + 512],
                in_=c.ab1[0:64, B_QPF + h * 512:B_QPF + (h + 1) * 512]),
                inc="st", amt=16)
        st_l = c.val("st")
        W("scalar", "st", st_l)
        last_strip_mm[0] = c.val("mm")



def emit_s2b(c):
    W, on, VS = c.W, c.on, c.VS
    REG = 16384
    AB2_P, AB2_PT, AB2_FEAT = 0, 2048, 4096
    ACC = (0, 128)
    (SL_L0, SL_L1, SL_NM0, SL_NM1, SL_T, SL_ED, SL_LSEG, SL_ED2,
     SL_LI) = (4 * i for i in range(9))

    head_last_pe_mm = [0, 0]
    feat_dma_by_par = [0, 0]

    def load_head(h):
        par = h % 2
        base = par * REG
        lsem = "ldx%d" % par
        if head_last_pe_mm[par]:
            W("sync", "mm", head_last_pe_mm[par])
        W("sync", lsem)
        on("sync", lambda e, h=h, base=base: e.dma_start(
            out=c.ab1[:, base:base + 4096],
            in_=c.d_qnope_fm[h * 128:(h + 1) * 128, :]), inc=lsem, amt=16)
        on("sync", lambda e, h=h, base=base: e.dma_start(
            out=c.ab1[:, base + 4096:base + 8192],
            in_=c.d_knope_fm[h * 128:(h + 1) * 128, :]), inc=lsem, amt=16)
        on("sync", lambda e, h=h, base=base: e.dma_start(
            out=bass.AP(c.ab1, base + 8192, [[32768, 128], [128, 32], [1, 128]]),
            in_=bass.AP(c.d_v_tm, h * (32 * 128 * 128),
                        [[128, 128], [128 * 128, 32], [1, 128]])),
            inc=lsem, amt=16)
        t = on("sync", lambda e, h=h, base=base: e.dma_start(
            out=c.ab1[0:64, base + 12288:base + 16384],
            in_=c.d_qpe_fm[h * 64:(h + 1) * 64, :]), inc=lsem, amt=16)
        return t

    W("sync", "st")
    W("sync", "mm")
    kpe_ld = on("sync", lambda e: e.dma_start(
        out=c.ab2[0:64, 8192:12288], in_=c.d_kpe_fm[:, :]), inc="ldk", amt=16)
    W("tensor", "ldk", kpe_ld)
    ld_t = {0: load_head(0), 1: load_head(1)}
    prev_exp = [0]
    prev_ptev = [0]
    prev_pt_mm = [0]
    prev_av_mm = [0]
    prev_merge = [0]
    prev_featev = [0]
    W("scalar", "mm")
    W("vector", "mm")
    W("tensor", "act")

    for h in range(H_MY):
        par = h % 2
        base = par * REG
        QN = lambda sl, base=base: c.ab1[:, base + sl[0]:base + sl[1]]
        W("tensor", "ldx%d" % par, ld_t[h])
        for qi in range(32):
            klen = (qi + 1) * 128
            qsl = (qi * 128, qi * 128 + 128)
            nseg = (klen + 2047) // 2048
            for sg in range(nseg):
                kbase = sg * 2048
                kw = min(2048, klen - kbase)
                first = (sg == 0)
                lastseg = (sg == nseg - 1)
                W("tensor", "act", prev_exp[0])
                nbl = (kw + 511) // 512
                for bl in range(nbl):
                    kb = bl * 512
                    wkb = min(512, kw - kb)
                    has_diag = lastseg and (bl == nbl - 1)
                    on("tensor", lambda e, base=base, qsl=qsl, kbase=kbase, kb=kb,
                       wkb=wkb: e.matmul(
                        c.p0[:, kb:kb + wkb],
                        c.ab1[:, base + qsl[0]:base + qsl[1]],
                        c.ab1[:, base + 4096 + kbase + kb:base + 4096 + kbase + kb + wkb],
                        start=True, stop=False))
                    on("tensor", lambda e, base=base, qsl=qsl, kbase=kbase, kb=kb,
                       wkb=wkb, hd=has_diag: e.matmul(
                        c.p0[:, kb:kb + wkb],
                        c.ab1[0:64, base + 12288 + qsl[0]:base + 12288 + qsl[1]],
                        c.ab2[0:64, 8192 + kbase + kb:8192 + kbase + kb + wkb],
                        start=False, stop=not hd))
                    if has_diag:
                        on("tensor", lambda e, kb=kb, wkb=wkb: e.matmul(
                            c.p0[:, kb + wkb - 128:kb + wkb],
                            c.CIDENT(), c.CCAUSAL(),
                            start=False, stop=True))
                mm_sc = c.MK("tensor", "mm")
                W("vector", "mm", mm_sc)
                VS(lambda e, kw=kw: e.tensor_reduce(
                    c.sm[:, SL_T:SL_T + 1], c.p0[:, 0:kw], axis=AX.X,
                    op=ALU.max, negate=True))
                nm_cur = SL_NM0 if (sg % 2 == 0) else SL_NM1
                nm_oth = SL_NM1 if (sg % 2 == 0) else SL_NM0
                if first:
                    nm_t = VS(lambda e, nm=nm_cur: e.tensor_copy(
                        c.sm[:, nm:nm + 1], c.sm[:, SL_T:SL_T + 1]))
                else:
                    nm_t = VS(lambda e, nm=nm_cur, no=nm_oth: e.tensor_tensor(
                        c.sm[:, nm:nm + 1], c.sm[:, no:no + 1],
                        c.sm[:, SL_T:SL_T + 1], ALU.min))
                    VS(lambda e, nm=nm_cur, no=nm_oth: e.tensor_tensor(
                        c.sm[:, SL_ED:SL_ED + 1], c.sm[:, nm:nm + 1],
                        c.sm[:, no:no + 1], ALU.subtract))
                W("scalar", "dve", c.val("dve"))
                if prev_pt_mm[0]:
                    W("scalar", "mm", prev_pt_mm[0])
                if not first:
                    ed_t = on("scalar", lambda e: e.activation(
                        c.sm[:, SL_ED2:SL_ED2 + 1], c.sm[:, SL_ED:SL_ED + 1],
                        AF.Exp), inc="act")
                exp_t = on("scalar", lambda e, kw=kw, nm=nm_cur: e.activation(
                    c.ab2[:, AB2_P:AB2_P + kw], c.p0[:, 0:kw], AF.Exp,
                    bias=c.sm[:, nm:nm + 1],
                    accum_out=c.sm[:, SL_LSEG:SL_LSEG + 1]), inc="act")
                prev_exp[0] = exp_t
                W("tensor", "act", exp_t)
                if prev_ptev[0]:
                    W("tensor", "act", prev_ptev[0])
                nj = kw // 128
                for j in range(nj):
                    pt = c.pt6 if (j % 16) < 8 else c.pt7
                    on("tensor", lambda e, j=j, pt=pt: e.transpose(
                        pt[:, (j % 8) * 128:(j % 8) * 128 + 128],
                        c.ab2[:, AB2_P + j * 128:AB2_P + (j + 1) * 128],
                        c.CIDENT()))
                mm_pt = c.MK("tensor", "mm")
                prev_pt_mm[0] = mm_pt
                W("scalar", "mm", mm_pt)
                if prev_av_mm[0]:
                    W("scalar", "mm", prev_av_mm[0])
                if nj <= 8:
                    ptev = on("scalar", lambda e, nj=nj: e.copy(
                        c.ab2[:, AB2_PT:AB2_PT + nj * 128], c.pt6[:, 0:nj * 128]),
                        inc="act")
                else:
                    on("scalar", lambda e: e.copy(
                        c.ab2[:, AB2_PT:AB2_PT + 1024], c.pt6[:, :]), inc="act")
                    ptev = on("scalar", lambda e, nj=nj: e.copy(
                        c.ab2[:, AB2_PT + 1024:AB2_PT + nj * 128],
                        c.pt7[:, 0:(nj - 8) * 128]), inc="act")
                prev_ptev[0] = ptev
                W("tensor", "act", ptev)
                if prev_merge[0]:
                    W("tensor", "dve", prev_merge[0])
                for j in range(nj):
                    on("tensor", lambda e, j=j, base=base, kbase=kbase, nj=nj: e.matmul(
                        c.p4[:, 0:128],
                        c.ab2[:, AB2_PT + j * 128:AB2_PT + (j + 1) * 128],
                        c.ab1[:, base + 8192 + kbase + j * 128:base + 8192 + kbase + (j + 1) * 128],
                        start=(j == 0), stop=(j == nj - 1)))
                av_mm = c.MK("tensor", "mm")
                prev_av_mm[0] = av_mm
                acc_cur = ACC[sg % 2]
                acc_oth = ACC[1 - sg % 2]
                W("vector", "mm", av_mm)
                if first:
                    VS(lambda e, a=acc_cur: e.tensor_copy(
                        c.af2[:, a:a + 128], c.p4[:, 0:128]))
                    mg = VS(lambda e: e.tensor_copy(
                        c.sm[:, SL_L0:SL_L0 + 1], c.sm[:, SL_LSEG:SL_LSEG + 1]))
                else:
                    W("vector", "act", exp_t)
                    VS(lambda e, a=acc_cur, ao=acc_oth: e.scalar_tensor_tensor(
                        c.af2[:, a:a + 128], c.af2[:, ao:ao + 128],
                        c.sm[:, SL_ED2:SL_ED2 + 1], c.p4[:, 0:128],
                        ALU.mult, ALU.add))
                    lc = SL_L0 if (sg % 2 == 0) else SL_L1
                    lo = SL_L1 if (sg % 2 == 0) else SL_L0
                    mg = VS(lambda e, lc=lc, lo=lo: e.scalar_tensor_tensor(
                        c.sm[:, lc:lc + 1], c.sm[:, lo:lo + 1],
                        c.sm[:, SL_ED2:SL_ED2 + 1],
                        c.sm[:, SL_LSEG:SL_LSEG + 1], ALU.mult, ALU.add))
                prev_merge[0] = mg
            lfin = SL_L0 if ((nseg - 1) % 2 == 0) else SL_L1
            acc_fin = ACC[(nseg - 1) % 2]
            VS(lambda e, lf=lfin: e.reciprocal(
                c.sm[:, SL_LI:SL_LI + 1], c.sm[:, lf:lf + 1]))
            ot = VS(lambda e, a=acc_fin: e.tensor_scalar(
                c.af2[:, 256:384].bitcast(BF16)[:, 0:128],
                c.af2[:, a:a + 128], c.sm[:, SL_LI:SL_LI + 1], None, ALU.mult))
            W("tensor", "dve", ot)
            W("tensor", "act", prev_featev[0])
            trf = on("tensor", lambda e: e.transpose(
                c.pt6[:, 0:128], c.af2[:, 256:384].bitcast(BF16)[:, 0:128],
                c.CIDENT()), inc="mm")
            W("scalar", "mm", trf)
            fev = on("scalar", lambda e, qi=qi: e.copy(
                c.ab2[:, AB2_FEAT + qi * 128:AB2_FEAT + (qi + 1) * 128],
                c.pt6[:, 0:128]), inc="act")
            prev_featev[0] = fev
        W("gpsimd", "act", prev_featev[0])
        if feat_dma_by_par[par]:
            pass
        fd = on("gpsimd", lambda e, h=h: e.dma_start(
            out=c.d_feat_fm[h * 128:(h + 1) * 128, :],
            in_=c.ab2[:, AB2_FEAT:AB2_FEAT + 4096]), inc="st", amt=16)
        feat_dma_by_par[par] = fd
        W("scalar", "st", fd)
        head_last_pe_mm[par] = c.val("mm")
        if h + 2 < H_MY:
            ld_t[h + 2] = load_head(h + 2)



def emit_s2c(c):
    W, on, VS = c.W, c.on, c.VS
    W("sync", "st")
    for f in range(8):
        lsem = "ldw%d" % (f % 2)
        buf = (f % 2) * 2048
        W("sync", "dve")
        W("sync", "mm")
        ldt = on("sync", lambda e, f=f, buf=buf: e.dma_start(
            out=c.af2[:, buf:buf + 2048], in_=c.w_o_my[f * 128:(f + 1) * 128, :]),
            inc=lsem, amt=16)
        W("vector", lsem, ldt)
        on("vector", lambda e, f=f, buf=buf: e.tensor_copy(
            c.ab2[:, f * 2048:(f + 1) * 2048], c.af2[:, buf:buf + 2048]), inc="dve")
    wo_ready = c.val("dve")
    W("tensor", "dve", wo_ready)

    prev_ev = [0]
    prev_st = [0]
    for strip in range(8):
        s0c = strip * 512
        lsem = "ldx%d" % (strip % 2)
        W("sync", lsem)
        W("sync", "mm")
        for f in range(8):
            on("sync", lambda e, f=f, s0c=s0c: e.dma_start(
                out=c.ab1[:, f * 512:(f + 1) * 512],
                in_=c.d_feat_fm[f * 128:(f + 1) * 128, s0c:s0c + 512]),
                inc=lsem, amt=16)
        ldt = c.val(lsem)
        W("tensor", lsem, ldt)
        for sub in range(4):
            t0 = s0c + sub * 128
            W("tensor", "act", prev_ev[0])
            for ch in range(4):
                for f in range(8):
                    on("tensor", lambda e, ch=ch, f=f, sub=sub: e.matmul(
                        c.p0[:, ch * 512:(ch + 1) * 512],
                        c.ab1[:, f * 512 + sub * 128:f * 512 + sub * 128 + 128],
                        c.ab2[:, f * 2048 + ch * 512:f * 2048 + (ch + 1) * 512],
                        start=(f == 0), stop=(f == 7)))
            mm_t = c.MK("tensor", "mm")
            W("scalar", "mm", mm_t)
            if prev_st[0]:
                W("scalar", "st", prev_st[0])
            pp = sub % 2
            ev = on("scalar", lambda e, pp=pp: e.copy(
                c.af1[:, pp * 2048:(pp + 1) * 2048], c.p0[:, 0:2048]), inc="act")
            prev_ev[0] = ev
            W("gpsimd", "act", ev)
            stt_ = on("gpsimd", lambda e, t0=t0, pp=pp: e.dma_start(
                out=c.rs_attn_in[t0:t0 + 128, :],
                in_=c.af1[:, pp * 2048:(pp + 1) * 2048]), inc="st", amt=16)
            if sub >= 1:
                prev_st[0] = stt_ - 16
    W("gpsimd", "st")
    on("gpsimd", lambda e: e.collective_compute(
        "ReduceScatter", ALU.add, replica_groups=c.RG,
        ins=[c.rs_attn_in.ap().opt()], outs=[c.rs_attn_out.ap().opt()]),
        inc="ccs", amt=1)
    W("gpsimd", "ccs")



def emit_s3(c):
    W, on, VS = c.W, c.on, c.VS
    W("sync", "ccs")
    W("sync", "mm")
    W("sync", "dve")
    for f in range(16):
        on("sync", lambda e, f=f: e.dma_start(
            out=c.ab1[:, f * 2048:(f + 1) * 2048],
            in_=c.d_bwT[f * 128:(f + 1) * 128, :]), inc="ldw0", amt=16)
    on("sync", lambda e: e.dma_start(
        out=c.af2[:, 2048:4096], in_=c.ffn_b[:, :]), inc="ldw0", amt=16)
    bw_ld = c.val("ldw0")
    W("tensor", "ldw0", bw_ld)
    W("vector", "ldw0", bw_ld)

    (SM_SX, SM_SX2, SM_M, SM_T1, SM_MSQ, SM_VAR, SM_REC, SM_RSTD) = (
        4 * i for i in range(8))
    prev_tr = [0]
    prev_ev = [0]
    prev_proj = [0]
    prev_ag_dma = [0]
    xsl = {}
    r2st = {}

    def ln_tile(src_fn, eps, dst_fn, tag):
        VS(lambda e: e.tensor_reduce(
            c.sm[:, SM_SX:SM_SX + 1], src_fn(), axis=AX.X, op=ALU.add))
        W("scalar", "dve")
        sq = on("scalar", lambda e: e.activation(
            c.ab2[:, 6144:8192], src_fn(), AF.Square,
            accum_out=c.sm[:, SM_SX2:SM_SX2 + 1]), inc="act")
        W("vector", "act", sq)
        VS(lambda e: e.tensor_scalar(
            c.sm[:, SM_M:SM_M + 1], c.sm[:, SM_SX:SM_SX + 1], 1.0 / D, None, ALU.mult))
        VS(lambda e: e.tensor_scalar(
            c.sm[:, SM_T1:SM_T1 + 1], c.sm[:, SM_SX2:SM_SX2 + 1], 1.0 / D, eps,
            ALU.mult, ALU.add))
        VS(lambda e: e.tensor_tensor(
            c.sm[:, SM_MSQ:SM_MSQ + 1], c.sm[:, SM_M:SM_M + 1],
            c.sm[:, SM_M:SM_M + 1], ALU.mult))
        VS(lambda e: e.tensor_tensor(
            c.sm[:, SM_VAR:SM_VAR + 1], c.sm[:, SM_T1:SM_T1 + 1],
            c.sm[:, SM_MSQ:SM_MSQ + 1], ALU.subtract))
        rec = VS(lambda e: e.reciprocal(
            c.sm[:, SM_REC:SM_REC + 1], c.sm[:, SM_VAR:SM_VAR + 1]))
        W("scalar", "dve", rec)
        rstd = on("scalar", lambda e: e.sqrt(
            c.sm[:, SM_RSTD:SM_RSTD + 1], c.sm[:, SM_REC:SM_REC + 1]), inc="act")
        W("vector", "act", rstd)
        VS(lambda e: e.tensor_scalar(
            c.af2[:, 0:2048], src_fn(), c.sm[:, SM_M:SM_M + 1], None, ALU.subtract))
        return VS(lambda e: e.tensor_scalar(
            dst_fn(), c.af2[:, 0:2048], c.sm[:, SM_RSTD:SM_RSTD + 1], None, ALU.mult))

    for t in range(NT_HALF):
        t0 = t * 128
        pp = t % 2
        lsem = "ldx%d" % pp
        if t >= 2 and xsl.get(pp):
            W("sync", "dve", xsl[pp])
        if r2st.get(pp):
            W("sync", "st", r2st[pp])
        W("sync", lsem)
        on("sync", lambda e, t0=t0, pp=pp: e.dma_start(
            out=c.af1[:, pp * 4096:pp * 4096 + 2048],
            in_=c.x_my[t0:t0 + 128, :]), inc=lsem, amt=16)
        ldt = on("sync", lambda e, t0=t0, pp=pp: e.dma_start(
            out=c.af1[:, pp * 4096 + 2048:pp * 4096 + 4096],
            in_=c.rs_attn_out[t0:t0 + 128, :]), inc=lsem, amt=16)
        W("vector", lsem, ldt)
        r1 = VS(lambda e, pp=pp: e.tensor_tensor(
            c.af3[:, pp * 2048:(pp + 1) * 2048],
            c.af1[:, pp * 4096:pp * 4096 + 2048],
            c.af1[:, pp * 4096 + 2048:pp * 4096 + 4096], ALU.add))
        xsl[pp] = r1
        r1f = lambda pp=pp: c.af3[:, pp * 2048:(pp + 1) * 2048]
        r2f = lambda pp=pp: c.af1[:, pp * 4096:pp * 4096 + 2048]
        if prev_tr[0]:
            W("vector", "mm", prev_tr[0])
        ln2 = ln_tile(r1f, LN_EPS, lambda: c.ab2[:, 0:2048], "ln2")
        W("tensor", "dve", ln2)
        W("tensor", "act", prev_ev[0])
        for fh, pt in ((0, c.pt6), (1, c.pt7)):
            for j in range(8):
                f = fh * 8 + j
                on("tensor", lambda e, f=f, j=j, pt=pt: e.transpose(
                    pt[:, j * 128:(j + 1) * 128], c.ab2[:, f * 128:(f + 1) * 128],
                    c.CIDENT()))
        tr_t = c.MK("tensor", "mm")
        prev_tr[0] = tr_t
        W("scalar", "mm", tr_t)
        ev1 = on("scalar", lambda e: e.copy(c.ab2[:, 2048:3072], c.pt6[:, :]), inc="act")
        ev2 = on("scalar", lambda e: e.copy(c.ab2[:, 3072:4096], c.pt7[:, :]), inc="act")
        prev_ev[0] = ev2
        W("tensor", "act", ev2)
        if prev_proj[0]:
            W("tensor", "dve", prev_proj[0])
        for ch in range(4):
            for f in range(16):
                on("tensor", lambda e, ch=ch, f=f: e.matmul(
                    c.p0[:, ch * 512:(ch + 1) * 512],
                    c.ab2[:, 2048 + f * 128:2048 + (f + 1) * 128],
                    c.ab1[:, f * 2048 + ch * 512:f * 2048 + (ch + 1) * 512],
                    start=(f == 0), stop=(f == 15)))
        mm_t = c.MK("tensor", "mm")
        W("vector", "mm", mm_t)
        VS(lambda e: e.scalar_tensor_tensor(
            c.af2[:, 0:2048], c.p0[:, 0:2048], 1.0, c.af2[:, 2048:4096],
            ALU.mult, ALU.add))
        r2 = VS(lambda e, pp=pp: e.tensor_tensor(
            c.af1[:, pp * 4096:pp * 4096 + 2048], c.af2[:, 0:2048],
            c.af3[:, pp * 2048:(pp + 1) * 2048], ALU.add))
        prev_proj[0] = r2
        xsl[pp] = r2
        W("gpsimd", "dve", r2)
        r2st[pp] = on("gpsimd", lambda e, t0=t0, pp=pp: e.dma_start(
            out=c.d_r2[t0:t0 + 128, :], in_=c.af1[:, pp * 4096:pp * 4096 + 2048]),
            inc="st", amt=16)
        if prev_ag_dma[0]:
            W("vector", "st", prev_ag_dma[0])
        ln3 = ln_tile(r2f, LN_EPS, lambda: c.ab2[:, 4096:6144], "ln3")
        xsl[pp] = ln3
        W("gpsimd", "dve", ln3)
        ag = on("gpsimd", lambda e, t0=t0: e.dma_start(
            out=c.ag_ln3_in[t0:t0 + 128, :], in_=c.ab2[:, 4096:6144]),
            inc="st", amt=16)
        prev_ag_dma[0] = ag
    W("gpsimd", "st")
    on("gpsimd", lambda e: e.collective_compute(
        "AllGather", ALU.bypass, replica_groups=c.RG,
        ins=[c.ag_ln3_in.ap().opt()], outs=[c.ag_ln3_out.ap().opt()]),
        inc="ccs", amt=1)
    W("gpsimd", "ccs")



def _load_w_bf16(c, wsrc, ntiles, wcols, dst_arena, dst_base, scale_col=None):
    W, on = c.W, c.on
    for f in range(ntiles):
        lsem = "ldw%d" % (f % 2)
        buf = (f % 2) * 2048
        W("sync", "dve")
        ldt = on("sync", lambda e, w=wsrc, f=f, buf=buf, wc=wcols: e.dma_start(
            out=c.af2[:, buf:buf + wc], in_=w[f * 128:(f + 1) * 128, :]),
            inc=lsem, amt=16)
        W("vector", lsem, ldt)
        if scale_col is None:
            on("vector", lambda e, f=f, buf=buf, wc=wcols, db=dst_base, da=dst_arena:
               e.tensor_copy(da[:, db + f * wc:db + (f + 1) * wc],
                             c.af2[:, buf:buf + wc]), inc="dve")
        else:
            on("vector", lambda e, f=f, buf=buf, wc=wcols, db=dst_base, da=dst_arena,
               sc=scale_col: e.tensor_scalar(
                da[:, db + f * wc:db + (f + 1) * wc], c.af2[:, buf:buf + wc],
                sc(f), None, ALU.mult), inc="dve")
    return c.val("dve")


def emit_s5(c):
    W, on, VS = c.W, c.on, c.VS

    def SS(fn):
        t = on("scalar", fn, inc="act")
        W("scalar", "act", t)
        return t

    W("sync", "ccs")
    W("sync", "mm")
    wz_rdy = _load_w_bf16(c, c.wz_my, 16, 2048, c.ab1, 0)
    W("sync", "dve")
    ldt = on("sync", lambda e: e.dma_start(
        out=c.af2[:, 0:512],
        in_=bass.AP(c.wdt_my, 0, [[32, 128], [4096, 16], [1, 32]])), inc="ldw0", amt=16)
    W("vector", "ldw0", ldt)
    wdt_rdy = on("vector", lambda e: e.tensor_copy(
        c.ab2[:, 23040:23552], c.af2[:, 0:512]), inc="dve")
    an1 = on("scalar", lambda e: e.activation(
        c.cstf[:, 1218:1250], c.ALOG(), AF.Exp), inc="act")
    W("vector", "act", an1)
    aneg_rdy = VS(lambda e: e.tensor_scalar(
        c.cstf[:, 1186:1218], c.cstf[:, 1218:1250], -1.0, None, ALU.mult))
    c.ANEG = lambda: c.cstf[:, 1186:1218]

    A_LN0, A_LN1, A_LNF, A_ZE = 16896, 18944, 20992, 14848
    prev_tr = [0]; prev_ev = [0]; prev_z_dma = [0]; prev_mm = [0]; prev_u = [0]
    W("tensor", "dve", wz_rdy)
    W("tensor", "dve", wdt_rdy)
    for t in range(NT_FULL):
        t0 = t * 128
        pp = t % 2
        lnslot = A_LN0 if pp == 0 else A_LN1
        lsem = "ldx%d" % pp
        W("sync", lsem)
        if prev_mm[0]:
            W("sync", "mm", prev_mm[0])
        ldt = on("sync", lambda e, t0=t0, sl=lnslot: e.dma_start(
            out=c.ab2[:, sl:sl + 2048], in_=c.ag_ln3_out[t0:t0 + 128, :]),
            inc=lsem, amt=16)
        W("tensor", lsem, ldt)
        W("tensor", "act", prev_ev[0])
        for fh, pt in ((0, c.pt6), (1, c.pt7)):
            for j in range(8):
                f = fh * 8 + j
                on("tensor", lambda e, f=f, j=j, pt=pt, sl=lnslot: e.transpose(
                    pt[:, j * 128:(j + 1) * 128],
                    c.ab2[:, sl + f * 128:sl + (f + 1) * 128], c.CIDENT()))
        tr_t = c.MK("tensor", "mm")
        prev_mm[0] = tr_t
        W("scalar", "mm", tr_t)
        ev1 = on("scalar", lambda e: e.copy(c.ab2[:, A_LNF:A_LNF + 1024], c.pt6[:, :]), inc="act")
        ev2 = on("scalar", lambda e: e.copy(c.ab2[:, A_LNF + 1024:A_LNF + 2048], c.pt7[:, :]), inc="act")
        prev_ev[0] = ev2
        W("tensor", "act", ev2)
        if prev_u[0]:
            W("tensor", "dve", prev_u[0])
        for f in range(16):
            on("tensor", lambda e, f=f: e.matmul(
                c.p5[:, 0:32],
                c.ab2[:, A_LNF + f * 128:A_LNF + (f + 1) * 128],
                c.ab2[:, 23040 + f * 32:23040 + (f + 1) * 32],
                start=(f == 0), stop=(f == 15)))
        for ch in range(4):
            for f in range(16):
                on("tensor", lambda e, ch=ch, f=f: e.matmul(
                    c.p0[:, ch * 512:(ch + 1) * 512],
                    c.ab2[:, A_LNF + f * 128:A_LNF + (f + 1) * 128],
                    c.ab1[:, f * 2048 + ch * 512:f * 2048 + (ch + 1) * 512],
                    start=(f == 0), stop=(f == 15)))
        mm_t = c.MK("tensor", "mm")
        prev_mm[0] = mm_t
        W("scalar", "mm", mm_t)
        if prev_z_dma[0]:
            W("scalar", "st", prev_z_dma[0])
        evz = on("scalar", lambda e: e.copy(c.ab2[:, A_ZE:A_ZE + 2048], c.p0[:, 0:2048]), inc="act")
        W("gpsimd", "act", evz)
        zd = on("gpsimd", lambda e, t0=t0: e.dma_start(
            out=c.d_z[t0:t0 + 128, :], in_=c.ab2[:, A_ZE:A_ZE + 2048]), inc="st", amt=16)
        prev_z_dma[0] = zd
        W("vector", "mm", mm_t)
        u = VS(lambda e: e.tensor_tensor(
            c.af2[:, 0:32], c.p5[:, 0:32], c.DTBIAS(), ALU.add))
        W("scalar", "dve", u)
        au = SS(lambda e: e.activation(c.af2[:, 32:64], c.af2[:, 0:32], AF.Abs))
        ex = SS(lambda e: e.activation(c.af2[:, 64:96], c.af2[:, 32:64], AF.Exp, scale=-1.0))
        lg = SS(lambda e: e.activation(c.af2[:, 96:128], c.af2[:, 64:96], AF.Ln, bias=1.0))
        W("vector", "act", lg)
        rl = VS(lambda e: e.tensor_scalar(
            c.af2[:, 128:160], c.af2[:, 0:32], 0.0, None, ALU.max))
        dt_t = VS(lambda e, t=t: e.tensor_tensor(
            c.af3[:, t * 32:(t + 1) * 32], c.af2[:, 128:160], c.af2[:, 96:128], ALU.add))
        a_t = VS(lambda e, t=t: e.tensor_tensor(
            c.af3[:, 1024 + t * 32:1024 + (t + 1) * 32],
            c.af3[:, t * 32:(t + 1) * 32], c.ANEG(), ALU.mult))
        prev_u[0] = u
        if STAGE_LIMIT == "s5dbg":
            W("gpsimd", "dve", a_t)
            on("gpsimd", lambda e, t=t: e.dma_start(
                out=c.d_g[t * 128:(t + 1) * 128, 0:160], in_=c.af2[:, 0:160]),
                inc="st", amt=16)
            on("gpsimd", lambda e, t=t: e.dma_start(
                out=c.d_g[t * 128:(t + 1) * 128, 160:672],
                in_=c.ab2[:, 23040:23552]), inc="st", amt=16)
            on("gpsimd", lambda e, t=t: e.dma_start(
                out=c.d_g[t * 128:(t + 1) * 128, 672:2048],
                in_=c.ab2[:, A_LNF:A_LNF + 1376]), inc="st", amt=16)

    W("sync", "mm")
    wx_rdy = _load_w_bf16(c, c.wx_my, 16, 2048, c.ab1, 0)
    wb_rdy = _load_w_bf16(c, c.wb_my, 16, 512, c.ab2, 0)
    wc_rdy = _load_w_bf16(c, c.wc_my, 16, 512, c.ab2, 8192)
    W("tensor", "dve", wc_rdy)
    xbct = c.af2[:, 2048:4096].bitcast(BF16)
    prev_tr2 = [0]; prev_ev2 = [0]; prev_xdma = [0]; prev_mm2 = [c.val("mm")]
    for t in range(NT_FULL):
        t0 = t * 128
        pp = t % 2
        lnslot = A_LN0 if pp == 0 else A_LN1
        lsem = "ldx%d" % pp
        W("sync", lsem)
        W("sync", "mm", prev_mm2[0])
        ldt = on("sync", lambda e, t0=t0, sl=lnslot: e.dma_start(
            out=c.ab2[:, sl:sl + 2048], in_=c.ag_ln3_out[t0:t0 + 128, :]),
            inc=lsem, amt=16)
        W("tensor", lsem, ldt)
        W("tensor", "act", prev_ev2[0])
        for fh, pt in ((0, c.pt6), (1, c.pt7)):
            for j in range(8):
                f = fh * 8 + j
                on("tensor", lambda e, f=f, j=j, pt=pt, sl=lnslot: e.transpose(
                    pt[:, j * 128:(j + 1) * 128],
                    c.ab2[:, sl + f * 128:sl + (f + 1) * 128], c.CIDENT()))
        tr_t = c.MK("tensor", "mm")
        W("scalar", "mm", tr_t)
        ev1 = on("scalar", lambda e: e.copy(c.ab2[:, A_LNF:A_LNF + 1024], c.pt6[:, :]), inc="act")
        ev2 = on("scalar", lambda e: e.copy(c.ab2[:, A_LNF + 1024:A_LNF + 2048], c.pt7[:, :]), inc="act")
        prev_ev2[0] = ev2
        W("tensor", "act", ev2)
        for ch in range(4):
            for f in range(16):
                on("tensor", lambda e, ch=ch, f=f: e.matmul(
                    c.p0[:, ch * 512:(ch + 1) * 512],
                    c.ab2[:, A_LNF + f * 128:A_LNF + (f + 1) * 128],
                    c.ab1[:, f * 2048 + ch * 512:f * 2048 + (ch + 1) * 512],
                    start=(f == 0), stop=(f == 15)))
        for f in range(16):
            on("tensor", lambda e, f=f: e.matmul(
                c.p4[:, 0:512], c.ab2[:, A_LNF + f * 128:A_LNF + (f + 1) * 128],
                c.ab2[:, f * 512:(f + 1) * 512], start=(f == 0), stop=(f == 15)))
        for f in range(16):
            on("tensor", lambda e, f=f: e.matmul(
                c.p5[:, 0:512], c.ab2[:, A_LNF + f * 128:A_LNF + (f + 1) * 128],
                c.ab2[:, 8192 + f * 512:8192 + (f + 1) * 512],
                start=(f == 0), stop=(f == 15)))
        mm_t = c.MK("tensor", "mm")
        W("scalar", "mm", mm_t)
        evx1 = on("scalar", lambda e, xbct=xbct: e.copy(xbct[:, 0:2048], c.p0[:, 0:2048]), inc="act")
        evx2 = on("scalar", lambda e, xbct=xbct: e.copy(xbct[:, 2048:2560], c.p4[:, 0:512]), inc="act")
        evx3 = on("scalar", lambda e, xbct=xbct: e.copy(xbct[:, 2560:3072], c.p5[:, 0:512]), inc="act")
        W("tensor", "act", evx3)
        for b in range(3):
            pt = (c.pt6, c.pt7, c.pt6)[b]
            if b == 2:
                W("tensor", "act")
            for j in range(8):
                f = b * 8 + j
                on("tensor", lambda e, f=f, j=j, pt=pt, xbct=xbct: e.transpose(
                    pt[:, j * 128:(j + 1) * 128],
                    xbct[:, f * 128:(f + 1) * 128], c.CIDENT()))
            trb = c.MK("tensor", "mm")
            W("scalar", "mm", trb)
            if b == 2 and prev_xdma[0]:
                W("scalar", "st", prev_xdma[0])
            evb = on("scalar", lambda e, b=b, pt=pt: e.copy(
                c.af1[:, 0:4096].bitcast(BF16)[:, b * 1024:(b + 1) * 1024],
                pt[:, :]), inc="act")
        prev_mm2[0] = c.val("mm")
        ev_all = c.val("act")
        W("gpsimd", "act", ev_all)
        for f in range(24):
            on("gpsimd", lambda e, f=f, t0=t0: e.dma_start(
                out=c.d_xbc[f * 128:(f + 1) * 128, t0:t0 + 128],
                in_=c.af1[:, 0:4096].bitcast(BF16)[:, f * 128:(f + 1) * 128]),
                inc="st", amt=16)
        prev_xdma[0] = c.val("st")



def emit_s6(c):
    W, on, VS = c.W, c.on, c.VS
    W("sync", "st")
    W("sync", "dve")
    ldw = on("sync", lambda e: e.dma_start(
        out=c.sm2[:, 128:224],
        in_=bass.AP(c.conv_w_my, 0, [[4, 128], [512, 24], [1, 4]])),
        inc="ldk", amt=16)
    prev = [0]
    for ct in range(24):
        pp = ct % 2
        lsem = "ldx%d" % pp
        W("sync", lsem)
        if prev[0]:
            W("sync", "dve", prev[0])
        ldt = on("sync", lambda e, ct=ct, pp=pp: e.dma_start(
            out=c.ab1[:, pp * 4096:(pp + 1) * 4096],
            in_=c.d_xbc[ct * 128:(ct + 1) * 128, :]), inc=lsem, amt=16)
        W("vector", lsem, ldt)
        W("vector", "ldk", c.val("ldk"))
        W("vector", "act")
        xin = lambda pp=pp: c.ab1[:, pp * 4096:(pp + 1) * 4096]
        wcol = lambda k, ct=ct: c.sm2[:, 128 + ct * 4 + k:128 + ct * 4 + k + 1]
        bcol = lambda ct=ct: c.CONVB(ct)
        a0 = lambda: c.af1[:, 0:4096]
        a1 = lambda: c.af1[:, 4096:8192]
        VS(lambda e, a0=a0, xin=xin, wcol=wcol: e.tensor_scalar(
            a0()[:, 0:4096], xin(), wcol(3), None, ALU.mult))
        VS(lambda e, a0=a0, a1=a1, bcol=bcol: e.tensor_scalar(
            a1()[:, 0:4096], a0()[:, 0:4096], bcol(), None, ALU.add))
        VS(lambda e, a0=a0, a1=a1, xin=xin, wcol=wcol: e.scalar_tensor_tensor(
            a0()[:, 1:4096], xin()[:, 0:4095], wcol(2), a1()[:, 1:4096],
            ALU.mult, ALU.add))
        VS(lambda e, a0=a0, a1=a1: e.tensor_copy(a0()[:, 0:1], a1()[:, 0:1]))
        VS(lambda e, a0=a0, a1=a1, xin=xin, wcol=wcol: e.scalar_tensor_tensor(
            a1()[:, 2:4096], xin()[:, 0:4094], wcol(1), a0()[:, 2:4096],
            ALU.mult, ALU.add))
        VS(lambda e, a0=a0, a1=a1: e.tensor_copy(a1()[:, 0:2], a0()[:, 0:2]))
        VS(lambda e, a0=a0, a1=a1, xin=xin, wcol=wcol: e.scalar_tensor_tensor(
            a0()[:, 3:4096], xin()[:, 0:4093], wcol(0), a1()[:, 3:4096],
            ALU.mult, ALU.add))
        cp = VS(lambda e, a0=a0, a1=a1: e.tensor_copy(a0()[:, 0:3], a1()[:, 0:3]))
        prev[0] = cp
        W("scalar", "dve", cp)
        W("scalar", "st")
        sl = on("scalar", lambda e, pp=pp: e.activation(
            c.ab2[:, 0:4096], c.af1[:, 0:4096], AF.Silu), inc="act")
        W("gpsimd", "act", sl)
        on("gpsimd", lambda e, ct=ct: e.dma_start(
            out=c.d_xbcs[ct * 128:(ct + 1) * 128, :], in_=c.ab2[:, 0:4096]),
            inc="st", amt=16)



def emit_s7(c):
    W, on, VS = c.W, c.on, c.VS
    NCH = 32
    XB0, XB1 = 0, 3072
    XTM, BTM = 6144, 8192
    DTX, XTL = 8704, 10752
    MEX, M0T = 12800, 14848
    CBT, RHU, IDD = 16896, 17408, 17920
    ONESB, SBF, ABF = 22016, 22144, 24192

    W("sync", "st")
    W("vector", "mm")
    W("vector", "act")
    W("vector", "st")
    on("vector", lambda e: e.memset(c.ab1[:, ONESB:ONESB + 128], 1.0))
    for h in range(NH_MY):
        on("vector", lambda e, h=h: e.tensor_scalar(
            c.ab1[:, IDD + h * 128:IDD + (h + 1) * 128], c.CIDENT(),
            c.DCOEF()[:, h:h + 1], None, ALU.mult))
    on("vector", lambda e: e.memset(c.af1[:, 0:2048], 0.0))
    init_t = VS(lambda e: e.memset(c.ab1[:, SBF:SBF + 2048], 0.0))

    def ldchunk(ct, wait_mm):
        pp = ct % 2
        base = XB0 if pp == 0 else XB1
        lsem = "ldx%d" % pp
        W("sync", lsem)
        if wait_mm:
            W("sync", "mm", wait_mm)
        for f in range(24):
            on("sync", lambda e, f=f, ct=ct, base=base: e.dma_start(
                out=c.ab1[:, base + f * 128:base + (f + 1) * 128],
                in_=c.d_xbcs[f * 128:(f + 1) * 128, ct * 128:(ct + 1) * 128]),
                inc=lsem, amt=16)
        return c.val(lsem)

    chunk_mm_done = [0, 0]
    ld = {0: ldchunk(0, 0), 1: ldchunk(1, 0)}
    prev_y_dve = [0]
    prev_sbf_dve = [init_t]
    prev_ydma = [0]
    for ct in range(NCH):
        pp = ct % 2
        base = XB0 if pp == 0 else XB1
        ping = ct % 2
        pong = 1 - ping
        xt = lambda f, base=base: c.ab1[:, base + f * 128:base + (f + 1) * 128]
        W("tensor", "ldx%d" % pp, ld[ct])
        W("vector", "mm")
        abf_t = VS(lambda e, ct=ct: e.tensor_copy(
            c.ab1[:, ABF:ABF + 32], c.af3[:, 1024 + ct * 32:1024 + (ct + 1) * 32]))
        W("tensor", "dve", abf_t)
        W("tensor", "act")
        on("tensor", lambda e: e.matmul(
            c.p4[:, 0:32], c.CUINCL(), c.ab1[:, ABF:ABF + 32], start=True, stop=True))
        on("tensor", lambda e: e.matmul(
            c.p4[:, 32:64], c.ab1[:, ONESB:ONESB + 128], c.ab1[:, ABF:ABF + 32],
            start=True, stop=True))
        for g in range(4):
            on("tensor", lambda e, g=g, xt=xt: e.matmul(
                c.p5[:, g * 128:(g + 1) * 128], xt(16 + g), xt(20 + g),
                start=(g == 0), stop=(g == 3)))
        for fh, pt in ((0, c.pt6), (1, c.pt7)):
            for j in range(8):
                f = fh * 8 + j
                on("tensor", lambda e, f=f, j=j, pt=pt, xt=xt: e.transpose(
                    pt[:, j * 128:(j + 1) * 128], xt(f), c.CIDENT()))
        mm1 = c.MK("tensor", "mm")
        W("scalar", "mm", mm1)
        e1 = on("scalar", lambda e: e.copy(c.sm2[:, 0:64], c.p4[:, 0:64]), inc="act")
        e2 = on("scalar", lambda e: e.copy(c.ab1[:, CBT:CBT + 512], c.p5[:, 0:512]), inc="act")
        e3 = on("scalar", lambda e: e.copy(c.ab1[:, XTM:XTM + 1024], c.pt6[:, :]), inc="act")
        e4 = on("scalar", lambda e: e.copy(c.ab1[:, XTM + 1024:XTM + 2048], c.pt7[:, :]), inc="act")
        el1 = on("scalar", lambda e: e.activation(
            c.sm2[:, 64:96], c.sm2[:, 0:32], AF.Exp), inc="act")
        W("scalar", "act", el1)
        el2 = on("scalar", lambda e: e.activation(
            c.sm2[:, 96:128], c.sm2[:, 32:64], AF.Exp), inc="act")
        W("vector", "act", el2)
        wsub = VS(lambda e: e.tensor_tensor(
            c.af2[:, 0:32], c.sm2[:, 32:64], c.sm2[:, 0:32], ALU.subtract))
        W("scalar", "dve", wsub)
        wcx = on("scalar", lambda e: e.activation(
            c.af2[:, 32:64], c.af2[:, 0:32], AF.Exp), inc="act")
        W("tensor", "act", e4)
        for g in range(4):
            on("tensor", lambda e, g=g, xt=xt: e.transpose(
                c.pt6[:, g * 128:(g + 1) * 128], xt(16 + g), c.CIDENT()))
        mm2 = c.MK("tensor", "mm")
        W("scalar", "mm", mm2)
        e5 = on("scalar", lambda e: e.copy(c.ab1[:, BTM:BTM + 512], c.pt6[:, 0:512]), inc="act")
        W("vector", "act", max(e4, wcx))
        dtv = c.af3[:, ct * 32:(ct + 1) * 32].unsqueeze(2).broadcast_to((128, 32, 64))
        x3 = c.ab1[:, XTM:XTM + 2048].rearrange("p (h z) -> p h z", h=32)
        d3 = c.ab1[:, DTX:DTX + 2048].rearrange("p (h z) -> p h z", h=32)
        xl3 = c.ab1[:, XTL:XTL + 2048].rearrange("p (h z) -> p h z", h=32)
        VS(lambda e, dtv=dtv, x3=x3, d3=d3: e.tensor_tensor(d3, x3, dtv, ALU.mult))
        wcv = c.af2[:, 32:64].unsqueeze(2).broadcast_to((128, 32, 64))
        xtl_t = VS(lambda e, wcv=wcv, d3=d3, xl3=xl3: e.tensor_tensor(
            xl3, d3, wcv, ALU.mult))

        mmd_prev = [0]
        m0_prev = [0]
        for g in range(4):
            gh0 = g * 8
            mex_t = 0
            for bb in range(2):
                if mmd_prev[0]:
                    W("vector", "mm", mmd_prev[0])
                for hh in range(4):
                    h = gh0 + bb * 4 + hh
                    VS(lambda e, h=h, hh=hh, ct=ct: e.tensor_scalar(
                        c.ab1[:, RHU + hh * 128:RHU + (hh + 1) * 128], c.CUINCL(),
                        c.af3[:, 1024 + ct * 32 + h:1024 + ct * 32 + h + 1],
                        None, ALU.mult))
                rhu_t = c.val("dve")
                W("tensor", "dve", rhu_t)
                W("tensor", "act")
                on("tensor", lambda e: e.matmul(
                    c.p4[:, 0:512], c.CSLOW(), c.ab1[:, RHU:RHU + 512],
                    start=True, stop=True))
                mmd = c.MK("tensor", "mm")
                mmd_prev[0] = mmd
                W("scalar", "mm", mmd)
                if m0_prev[0]:
                    W("scalar", "dve", m0_prev[0])
                mex_t = on("scalar", lambda e, bb=bb: e.activation(
                    c.ab1[:, MEX + bb * 512:MEX + (bb + 1) * 512], c.p4[:, 0:512],
                    AF.Exp), inc="act")
            W("vector", "act", mex_t)
            cbv = c.ab1[:, CBT + g * 128:CBT + (g + 1) * 128].unsqueeze(1) \
                .broadcast_to((128, 8, 128))
            uv = c.CUINCL().unsqueeze(1).broadcast_to((128, 8, 128))
            me3 = c.ab1[:, MEX:MEX + 1024].rearrange("p (h z) -> p h z", h=8)
            mo3 = c.ab1[:, M0T:M0T + 1024].rearrange("p (h z) -> p h z", h=8)
            mo3b = c.ab1[:, 24256:25280].rearrange("p (h z) -> p h z", h=8)
            VS(lambda e, uv=uv, me3=me3, mo3=mo3: e.tensor_tensor(
                mo3, me3, uv, ALU.mult))
            m0_t = VS(lambda e, cbv=cbv, mo3=mo3, mo3b=mo3b: e.tensor_tensor(
                mo3b, mo3, cbv, ALU.mult))
            m0_prev[0] = m0_t
            W("tensor", "dve", m0_t)
            if prev_y_dve[0]:
                W("tensor", "dve", prev_y_dve[0])
            for j in range(8):
                h = gh0 + j
                on("tensor", lambda e, j=j, h=h, g=g: e.matmul(
                    c.p0[:, g * 512 + j * 64:g * 512 + (j + 1) * 64],
                    c.ab1[:, 24256 + j * 128:24256 + (j + 1) * 128],
                    c.ab1[:, DTX + h * 64:DTX + (h + 1) * 64],
                    start=(j == 0), stop=False))
            for j in range(8):
                h = gh0 + j
                on("tensor", lambda e, j=j, h=h, g=g: e.matmul(
                    c.p0[:, g * 512 + j * 64:g * 512 + (j + 1) * 64],
                    c.ab1[:, IDD + h * 128:IDD + (h + 1) * 128],
                    c.ab1[:, XTM + h * 64:XTM + (h + 1) * 64],
                    start=False, stop=(j == 7)))
            W("tensor", "act", e2)
            W("tensor", "dve", c.val("dve"))
            on("tensor", lambda e, g=g, xt=xt: e.matmul(
                c.p5[:, 0:512], xt(20 + g),
                c.ab1[:, SBF + g * 512:SBF + (g + 1) * 512],
                start=True, stop=True))
            on("tensor", lambda e, g=g, gh0=gh0: e.matmul(
                c.p4[:, 0:512], c.ab1[:, BTM + g * 128:BTM + (g + 1) * 128],
                c.ab1[:, XTL + gh0 * 64:(XTL + (gh0 + 8) * 64)],
                start=True, stop=True))
            mm_y = c.MK("tensor", "mm")
            W("vector", "mm", mm_y)
            elv = c.sm2[:, 64 + g * 8:64 + (g + 1) * 8].unsqueeze(2) \
                .broadcast_to((128, 8, 64))
            p5v = c.p5[:, 0:512].rearrange("p (h z) -> p h z", h=8)
            t1v = c.af2[:, 64:576].rearrange("p (h z) -> p h z", h=8)
            VS(lambda e, p5v=p5v, elv=elv, t1v=t1v: e.tensor_tensor(
                t1v, p5v, elv, ALU.mult))
            yg = VS(lambda e, g=g: e.scalar_tensor_tensor(
                c.af1[:, 4096 + g * 512:4096 + (g + 1) * 512],
                c.p0[:, g * 512:(g + 1) * 512], 1.0, c.af2[:, 64:576],
                ALU.mult, ALU.add))
            elnv = c.sm2[:, 96 + g * 8:96 + (g + 1) * 8].unsqueeze(2) \
                .broadcast_to((128, 8, 64))
            sov = c.af1[:, ping * 2048 + g * 512:ping * 2048 + (g + 1) * 512] \
                .rearrange("p (h z) -> p h z", h=8)
            t2v = c.af2[:, 576:1088].rearrange("p (h z) -> p h z", h=8)
            VS(lambda e, sov=sov, elnv=elnv, t2v=t2v: e.tensor_tensor(
                t2v, sov, elnv, ALU.mult))
            sn = VS(lambda e, g=g, pong=pong: e.tensor_tensor(
                c.af1[:, pong * 2048 + g * 512:pong * 2048 + (g + 1) * 512],
                c.af2[:, 576:1088], c.p4[:, 0:512], ALU.add))
            sb = VS(lambda e, g=g, pong=pong: e.tensor_copy(
                c.ab1[:, SBF + g * 512:SBF + (g + 1) * 512],
                c.af1[:, pong * 2048 + g * 512:pong * 2048 + (g + 1) * 512]))
            prev_y_dve[0] = yg
        ylast = c.val("dve")
        W("gpsimd", "dve", ylast)
        if prev_ydma[0]:
            W("vector", "st", prev_ydma[0])
        yd = on("gpsimd", lambda e, ct=ct: e.dma_start(
            out=c.d_y[ct * 128:(ct + 1) * 128, :], in_=c.af1[:, 4096:6144]),
            inc="st", amt=16)
        prev_ydma[0] = yd
        chunk_mm_done[pp] = c.val("mm")
        if ct + 2 < NCH:
            ld[ct + 2] = ldchunk(ct + 2, chunk_mm_done[pp])



def emit_s8(c):
    W, on, VS = c.W, c.on, c.VS
    W("sync", "st")
    W("sync", "mm")
    W("sync", "dve")
    wout_rdy = _load_w_bf16(c, c.wout_my, 16, 2048, c.ab1, 0, scale_col=c.NORMW)
    prev_g_dma = [0]
    prev_var_dma = [0]
    for t in range(NT_FULL):
        t0 = t * 128
        pp = t % 2
        lsem = "ldx%d" % pp
        W("sync", lsem)
        W("sync", "act")
        W("sync", "dve")
        on("sync", lambda e, t0=t0, pp=pp: e.dma_start(
            out=c.af1[:, pp * 2048:(pp + 1) * 2048], in_=c.d_y[t0:t0 + 128, :]),
            inc=lsem, amt=16)
        ldt = on("sync", lambda e, t0=t0, pp=pp: e.dma_start(
            out=c.ab2[:, pp * 2048:(pp + 1) * 2048], in_=c.d_z[t0:t0 + 128, :]),
            inc=lsem, amt=16)
        W("scalar", lsem, ldt)
        sl = on("scalar", lambda e, pp=pp: e.activation(
            c.ab2[:, 4096 + pp * 2048:4096 + (pp + 1) * 2048],
            c.ab2[:, pp * 2048:(pp + 1) * 2048], AF.Silu), inc="act")
        W("vector", "act", sl)
        W("vector", "st", prev_g_dma[0])
        gt = VS(lambda e, pp=pp: e.tensor_tensor(
            c.af1[:, 4096 + pp * 2048:4096 + (pp + 1) * 2048],
            c.af1[:, pp * 2048:(pp + 1) * 2048],
            c.ab2[:, 4096 + pp * 2048:4096 + (pp + 1) * 2048], ALU.mult))
        W("scalar", "dve", gt)
        vp = on("scalar", lambda e, pp=pp: e.activation(
            c.ab2[:, 8192:10240], c.af1[:, 4096 + pp * 2048:4096 + (pp + 1) * 2048],
            AF.Square, accum_out=c.sm[:, 0:1]), inc="act")
        W("gpsimd", "act", vp)
        gd = on("gpsimd", lambda e, t0=t0, pp=pp: e.dma_start(
            out=c.d_g[t0:t0 + 128, :],
            in_=c.af1[:, 4096 + pp * 2048:4096 + (pp + 1) * 2048]), inc="st", amt=16)
        prev_g_dma[0] = gd
        vd = on("gpsimd", lambda e, t0=t0: e.dma_start(
            out=c.ar_var_in[t0:t0 + 128, :], in_=c.sm[:, 0:1]), inc="st", amt=16)
        prev_var_dma[0] = vd
        W("scalar", "st", vd)
    W("gpsimd", "st")
    on("gpsimd", lambda e: e.collective_compute(
        "AllReduce", ALU.add, replica_groups=c.RG,
        ins=[c.ar_var_in.ap().opt()], outs=[c.ar_var_out.ap().opt()]),
        inc="ccs", amt=1)
    W("gpsimd", "ccs")
    W("sync", "ccs")
    W("tensor", "dve", wout_rdy)
    prev_ev = [0]; prev_st = [0]; prev_tr = [0]
    for t in range(NT_FULL):
        t0 = t * 128
        pp = t % 2
        lsem = "ldx%d" % pp
        W("sync", lsem)
        W("sync", "mm")
        on("sync", lambda e, t0=t0, pp=pp: e.dma_start(
            out=c.af1[:, pp * 2048:(pp + 1) * 2048], in_=c.d_g[t0:t0 + 128, :]),
            inc=lsem, amt=16)
        ldt = on("sync", lambda e, t0=t0: e.dma_start(
            out=c.sm[:, 4:5], in_=c.ar_var_out[t0:t0 + 128, :]), inc=lsem, amt=16)
        W("vector", lsem, ldt)
        VS(lambda e: e.tensor_scalar(
            c.sm[:, 8:9], c.sm[:, 4:5], 1.0 / INTER, MAMBA_EPS, ALU.mult, ALU.add))
        rec = VS(lambda e: e.reciprocal(c.sm[:, 12:13], c.sm[:, 8:9]))
        W("scalar", "dve", rec)
        rst = on("scalar", lambda e: e.sqrt(c.sm[:, 16:17], c.sm[:, 12:13]), inc="act")
        W("vector", "act", rst)
        if prev_tr[0]:
            W("vector", "mm", prev_tr[0])
        gn = VS(lambda e, pp=pp: e.tensor_scalar(
            c.ab2[:, 0:2048], c.af1[:, pp * 2048:(pp + 1) * 2048],
            c.sm[:, 16:17], None, ALU.mult))
        W("tensor", "dve", gn)
        W("tensor", "act", prev_ev[0])
        for fh, pt in ((0, c.pt6), (1, c.pt7)):
            for j in range(8):
                f = fh * 8 + j
                on("tensor", lambda e, f=f, j=j, pt=pt: e.transpose(
                    pt[:, j * 128:(j + 1) * 128], c.ab2[:, f * 128:(f + 1) * 128],
                    c.CIDENT()))
        tr_t = c.MK("tensor", "mm")
        prev_tr[0] = tr_t
        W("scalar", "mm", tr_t)
        ev1 = on("scalar", lambda e: e.copy(c.ab2[:, 2048:3072], c.pt6[:, :]), inc="act")
        ev2 = on("scalar", lambda e: e.copy(c.ab2[:, 3072:4096], c.pt7[:, :]), inc="act")
        prev_ev[0] = ev2
        W("tensor", "act", ev2)
        for ch in range(4):
            for f in range(16):
                on("tensor", lambda e, ch=ch, f=f: e.matmul(
                    c.p0[:, ch * 512:(ch + 1) * 512],
                    c.ab2[:, 2048 + f * 128:2048 + (f + 1) * 128],
                    c.ab1[:, f * 2048 + ch * 512:f * 2048 + (ch + 1) * 512],
                    start=(f == 0), stop=(f == 15)))
        mm_t = c.MK("tensor", "mm")
        W("scalar", "mm", mm_t)
        if prev_st[0]:
            W("scalar", "st", prev_st[0])
        ev = on("scalar", lambda e, pp=pp: e.copy(
            c.af1[:, 4096 + pp * 2048:4096 + (pp + 1) * 2048], c.p0[:, 0:2048]),
            inc="act")
        W("gpsimd", "act", ev)
        st_ = on("gpsimd", lambda e, t0=t0, pp=pp: e.dma_start(
            out=c.rs_mam_in[t0:t0 + 128, :],
            in_=c.af1[:, 4096 + pp * 2048:4096 + (pp + 1) * 2048]), inc="st", amt=16)
        prev_st[0] = st_ - 16
    W("gpsimd", "st")
    on("gpsimd", lambda e: e.collective_compute(
        "ReduceScatter", ALU.add, replica_groups=c.RG,
        ins=[c.rs_mam_in.ap().opt()], outs=[c.rs_mam_out.ap().opt()]),
        inc="ccs", amt=1)
    W("gpsimd", "ccs")



def emit_s9(c):
    W, on, VS = c.W, c.on, c.VS
    W("sync", "st")
    W("sync", "ccs")
    W("sync", "mm")
    W("sync", "dve")
    for f in range(16):
        on("sync", lambda e, f=f: e.dma_start(
            out=c.ab1[:, f * 2048:(f + 1) * 2048],
            in_=c.d_bwT[f * 128:(f + 1) * 128, :]), inc="ldw0", amt=16)
    on("sync", lambda e: e.dma_start(
        out=c.af2[:, 2048:4096], in_=c.ffn_b[:, :]), inc="ldw0", amt=16)
    bw_ld = c.val("ldw0")
    W("tensor", "ldw0", bw_ld)
    W("vector", "ldw0", bw_ld)
    (SM_SX, SM_SX2, SM_M, SM_T1, SM_MSQ, SM_VAR, SM_REC, SM_RSTD) = (
        4 * i for i in range(8))
    prev_tr = [0]; prev_ev = [0]; prev_out = [0]
    xsl = {}
    outst = {}
    for t in range(NT_HALF):
        t0 = t * 128
        pp = t % 2
        lsem = "ldx%d" % pp
        if xsl.get(pp):
            W("sync", "dve", xsl[pp])
        if outst.get(pp):
            W("sync", "st", outst[pp])
        W("sync", lsem)
        on("sync", lambda e, t0=t0, pp=pp: e.dma_start(
            out=c.af1[:, pp * 4096:pp * 4096 + 2048], in_=c.d_r2[t0:t0 + 128, :]),
            inc=lsem, amt=16)
        ldt = on("sync", lambda e, t0=t0, pp=pp: e.dma_start(
            out=c.af1[:, pp * 4096 + 2048:pp * 4096 + 4096],
            in_=c.rs_mam_out[t0:t0 + 128, :]), inc=lsem, amt=16)
        W("vector", lsem, ldt)
        r3 = VS(lambda e, pp=pp: e.tensor_tensor(
            c.af3[:, pp * 2048:(pp + 1) * 2048],
            c.af1[:, pp * 4096:pp * 4096 + 2048],
            c.af1[:, pp * 4096 + 2048:pp * 4096 + 4096], ALU.add))
        xsl[pp] = r3
        r3f = lambda pp=pp: c.af3[:, pp * 2048:(pp + 1) * 2048]
        VS(lambda e, r3f=r3f: e.tensor_reduce(
            c.sm[:, SM_SX:SM_SX + 1], r3f(), axis=AX.X, op=ALU.add))
        W("scalar", "dve")
        sq = on("scalar", lambda e, r3f=r3f: e.activation(
            c.ab2[:, 6144:8192], r3f(), AF.Square,
            accum_out=c.sm[:, SM_SX2:SM_SX2 + 1]), inc="act")
        W("vector", "act", sq)
        VS(lambda e: e.tensor_scalar(
            c.sm[:, SM_M:SM_M + 1], c.sm[:, SM_SX:SM_SX + 1], 1.0 / D, None, ALU.mult))
        VS(lambda e: e.tensor_scalar(
            c.sm[:, SM_T1:SM_T1 + 1], c.sm[:, SM_SX2:SM_SX2 + 1], 1.0 / D, LN_EPS,
            ALU.mult, ALU.add))
        VS(lambda e: e.tensor_tensor(
            c.sm[:, SM_MSQ:SM_MSQ + 1], c.sm[:, SM_M:SM_M + 1],
            c.sm[:, SM_M:SM_M + 1], ALU.mult))
        VS(lambda e: e.tensor_tensor(
            c.sm[:, SM_VAR:SM_VAR + 1], c.sm[:, SM_T1:SM_T1 + 1],
            c.sm[:, SM_MSQ:SM_MSQ + 1], ALU.subtract))
        rec = VS(lambda e: e.reciprocal(
            c.sm[:, SM_REC:SM_REC + 1], c.sm[:, SM_VAR:SM_VAR + 1]))
        W("scalar", "dve", rec)
        rstd = on("scalar", lambda e: e.sqrt(
            c.sm[:, SM_RSTD:SM_RSTD + 1], c.sm[:, SM_REC:SM_REC + 1]), inc="act")
        W("vector", "act", rstd)
        VS(lambda e, r3f=r3f: e.tensor_scalar(
            c.af2[:, 0:2048], r3f(), c.sm[:, SM_M:SM_M + 1], None, ALU.subtract))
        if prev_tr[0]:
            W("vector", "mm", prev_tr[0])
        ln4 = VS(lambda e: e.tensor_scalar(
            c.ab2[:, 0:2048], c.af2[:, 0:2048], c.sm[:, SM_RSTD:SM_RSTD + 1],
            None, ALU.mult))
        W("tensor", "dve", ln4)
        W("tensor", "act", prev_ev[0])
        for fh, pt in ((0, c.pt6), (1, c.pt7)):
            for j in range(8):
                f = fh * 8 + j
                on("tensor", lambda e, f=f, j=j, pt=pt: e.transpose(
                    pt[:, j * 128:(j + 1) * 128], c.ab2[:, f * 128:(f + 1) * 128],
                    c.CIDENT()))
        tr_t = c.MK("tensor", "mm")
        prev_tr[0] = tr_t
        W("scalar", "mm", tr_t)
        ev1 = on("scalar", lambda e: e.copy(c.ab2[:, 2048:3072], c.pt6[:, :]), inc="act")
        ev2 = on("scalar", lambda e: e.copy(c.ab2[:, 3072:4096], c.pt7[:, :]), inc="act")
        prev_ev[0] = ev2
        W("tensor", "act", ev2)
        if xsl.get(1 - pp):
            W("tensor", "dve", xsl[1 - pp])
        for ch in range(4):
            for f in range(16):
                on("tensor", lambda e, ch=ch, f=f: e.matmul(
                    c.p0[:, ch * 512:(ch + 1) * 512],
                    c.ab2[:, 2048 + f * 128:2048 + (f + 1) * 128],
                    c.ab1[:, f * 2048 + ch * 512:f * 2048 + (ch + 1) * 512],
                    start=(f == 0), stop=(f == 15)))
        mm_t = c.MK("tensor", "mm")
        W("vector", "mm", mm_t)
        VS(lambda e: e.scalar_tensor_tensor(
            c.af2[:, 0:2048], c.p0[:, 0:2048], 1.0, c.af2[:, 2048:4096],
            ALU.mult, ALU.add))
        fin = VS(lambda e, pp=pp, r3f=r3f: e.tensor_tensor(
            c.af1[:, pp * 4096:pp * 4096 + 2048], c.af2[:, 0:2048], r3f(), ALU.add))
        xsl[pp] = fin
        W("gpsimd", "dve", fin)
        if S9_DEBUG:
            on("gpsimd", lambda e, t0=t0, r3f=r3f: e.dma_start(
                out=c.d_g[t0:t0 + 128, :], in_=r3f()), inc="st", amt=16)
            on("gpsimd", lambda e, t0=t0: e.dma_start(
                out=c.rs_attn_in[t0:t0 + 128, :], in_=c.af2[:, 0:2048]),
                inc="st", amt=16)
            dbg_t = on("gpsimd", lambda e, t0=t0: e.dma_start(
                out=c.d_y[t0:t0 + 128, 0:1024],
                in_=c.ab2[:, 0:2048].bitcast(F32)[:, 0:1024]), inc="st", amt=16)
            W("vector", "st", dbg_t)
            W("tensor", "st", dbg_t)
            W("scalar", "st", dbg_t)
        od = on("gpsimd", lambda e, t0=t0, pp=pp: e.dma_start(
            out=c.out[t0:t0 + 128, :], in_=c.af1[:, pp * 4096:pp * 4096 + 2048]),
            inc="st", amt=16)
        prev_out[0] = od
        outst[pp] = od




def _host_constants():
    ident = np.eye(128, dtype=np.float32)
    causal = np.triu(np.full((128, 128), -1e9, dtype=np.float32), 1)
    idx = np.arange(128)
    uincl = (idx[:, None] <= idx[None, :]).astype(np.float32)
    slow = (idx[:, None] > idx[None, :]).astype(np.float32)
    negid = np.tile(-1e4 * ident, (1, 4)).astype(np.float32)
    inv_freq = 1.0 / (10000.0 ** (np.arange(0, ROPE, 2, dtype=np.float32) / ROPE))
    t = np.arange(S, dtype=np.float32)
    freqs = np.outer(t, inv_freq)
    return dict(c_ident=ident, c_causal=causal, c_uincl=uincl, c_slow=slow,
                c_negid=negid, c_cos=np.cos(freqs).astype(np.float32),
                c_sin=np.sin(freqs).astype(np.float32))


def _col_fold(v, width=128):
    return np.ascontiguousarray(v.reshape(-1, width).T)


def make_in_maps(ins):
    x = np.asarray(ins["L0_attn_a_x"], dtype=np.float32)
    consts = _host_constants()
    w_uq = np.asarray(ins["L0_attn_a_w_uq"], dtype=np.float32)
    uq = w_uq.reshape(QL, 16, QKD)
    wdt = np.asarray(ins["L0_attn_b_Wdt"], dtype=np.float32)
    conv_w = np.asarray(ins["L0_attn_b_conv_w"], dtype=np.float32)
    conv_b = np.asarray(ins["L0_attn_b_conv_b"], dtype=np.float32)
    f32 = lambda k: np.asarray(ins[k], dtype=np.float32)
    in_maps = []
    for core in range(8):
        b, par = core // 2, core % 2
        heads = slice(par * H_MY, (par + 1) * H_MY)
        ih = slice(par * MYI, (par + 1) * MYI)
        gh = slice(par * MYGN, (par + 1) * MYGN)
        mh = slice(par * NH_MY, (par + 1) * NH_MY)
        cw = np.concatenate([
            conv_w[par * MYI:(par + 1) * MYI],
            conv_w[INTER + par * MYGN:INTER + (par + 1) * MYGN],
            conv_w[INTER + 1024 + par * MYGN:INTER + 1024 + (par + 1) * MYGN]], axis=0)
        cb = np.concatenate([
            conv_b[par * MYI:(par + 1) * MYI],
            conv_b[INTER + par * MYGN:INTER + (par + 1) * MYGN],
            conv_b[INTER + 1024 + par * MYGN:INTER + 1024 + (par + 1) * MYGN]], axis=0)
        m = dict(
            x_full=np.ascontiguousarray(x[b]),
            x_my=np.ascontiguousarray(x[b, par * HALF:(par + 1) * HALF]),
            w1=np.ascontiguousarray(np.concatenate(
                [f32("L0_attn_a_w_dq"), f32("L0_attn_a_w_kr"),
                 f32("L0_attn_a_w_dkv")], axis=1)),
            g_q=_col_fold(f32("L0_attn_a_g_q")),
            g_kv=_col_fold(f32("L0_attn_a_g_kv")),
            w_uq_nope=np.ascontiguousarray(
                uq[:, heads, :NOPE].reshape(QL, H_MY * NOPE)),
            w_uq_pe=np.ascontiguousarray(
                uq[:, heads, NOPE:].reshape(QL, H_MY * ROPE)),
            w_uk=np.ascontiguousarray(f32("L0_attn_a_w_uk").reshape(
                KVL, 16, NOPE)[:, heads].reshape(KVL, H_MY * NOPE)),
            w_uv=np.ascontiguousarray(f32("L0_attn_a_w_uv").reshape(
                KVL, 16, VD)[:, heads].reshape(KVL, H_MY * VD)),
            w_o_my=np.ascontiguousarray(
                f32("L0_attn_a_w_o")[par * H_MY * VD:(par + 1) * H_MY * VD]),
            ffn_w=np.ascontiguousarray(f32("L0_ffn_weight")),
            ffn_b=np.ascontiguousarray(np.tile(f32("L0_ffn_bias")[None, :], (128, 1))),
            wz_my=np.ascontiguousarray(f32("L0_attn_b_Wz")[:, ih]),
            wx_my=np.ascontiguousarray(f32("L0_attn_b_Wx")[:, ih]),
            wb_my=np.ascontiguousarray(f32("L0_attn_b_WB")[:, gh]),
            wc_my=np.ascontiguousarray(f32("L0_attn_b_WC")[:, gh]),
            wdt_my=np.ascontiguousarray(wdt[:, mh]),
            conv_w_my=np.ascontiguousarray(cw),
            conv_b_my=_col_fold(cb),
            dt_bias_t=np.ascontiguousarray(
                np.tile(f32("L0_attn_b_dt_bias")[mh][None, :], (128, 1))),
            a_log_t=np.ascontiguousarray(
                np.tile(f32("L0_attn_b_A_log")[mh][None, :], (128, 1))),
            d_coef_t=np.ascontiguousarray(
                np.tile(f32("L0_attn_b_D")[mh][None, :], (128, 1))),
            norm_w_my=_col_fold(f32("L0_attn_b_norm_w")[ih]),
            wout_my=np.ascontiguousarray(f32("L0_attn_b_Wout")[ih]),
        )
        m.update(consts)
        in_maps.append(m)
    return in_maps


def kernel(**ins):
    nc = build_program()
    in_maps = make_in_maps(ins)
    res = run_bass_kernel_spmd(nc, in_maps, core_ids=list(range(8)))
    full = np.empty((4, S, D), dtype=np.float32)
    for core in range(8):
        b, par = core // 2, core % 2
        full[b, par * HALF:(par + 1) * HALF] = res.results[core]["out"]
    return full
